# version 11
# baseline (speedup 1.0000x reference)
import contextlib
import numpy as np
import ml_dtypes
import concourse.bass as bass
import concourse.mybir as mybir
from concourse.bass_utils import run_bass_kernel_spmd

F32 = mybir.dt.float32
BF16 = mybir.dt.bfloat16
AF = mybir.ActivationFunctionType
ALU = mybir.AluOpType
AX = mybir.AxisListType

N_PH = 99
DEBUG = False
FFN_SWEEPS = 4
FFN_NE = 8
FFN_GATE_ONLY = False

D = 1024
NCH = 8
TF = 4096
TO = 2048
NHP = 8
EH = 64
HID = 2048
NE = 8
EPS = 1e-5
ATT_SCALE = 0.125
ATTB_SCALE = 1.0 / 64.0

ENGINES = ("tensor", "vector", "scalar", "gpsimd", "sync")


class Prog:
    def __init__(self):
        self.ops = {e: [] for e in ENGINES}
        self.sem_vals = {}
        self.last_wait = {e: {} for e in ENGINES}

    def _sem(self, name):
        if name not in self.sem_vals:
            self.sem_vals[name] = 0

    def emit(self, eng, fn, wait=(), inc=None, inc_n=1):
        waits = []
        lw = self.last_wait[eng]
        for t in wait:
            if t is None:
                continue
            s, v = t
            if v <= 0:
                continue
            if s in lw and lw[s] >= v:
                continue
            merged = False
            for i, (s2, v2) in enumerate(waits):
                if s2 == s:
                    waits[i] = (s, max(v, v2))
                    merged = True
            if not merged:
                waits.append((s, v))
        for s, v in waits:
            lw[s] = max(lw.get(s, 0), v)
        tok = None
        incs = []
        if inc is not None:
            self._sem(inc)
            self.sem_vals[inc] += inc_n
            tok = (inc, self.sem_vals[inc])
            incs.append((inc, inc_n))
        self.ops[eng].append((fn, waits, incs))
        return tok

    def dma(self, eng, fn, wait=(), inc=None):
        return self.emit(eng, fn, wait=wait, inc=inc, inc_n=16)

    def tok(self, sem):
        self._sem(sem)
        return (sem, self.sem_vals[sem])

    def build(self, nc, stack):
        sems = {name: stack.enter_context(nc.semaphore(name))
                for name in self.sem_vals}
        block = stack.enter_context(nc.Block())

        def mk(eng_name):
            def body(engine):
                for fn, waits, incs in self.ops[eng_name]:
                    for s, v in waits:
                        engine.wait_ge(sems[s], v)
                    inst = fn(engine)
                    for s, n in incs:
                        inst = inst.then_inc(sems[s], n)
            return body

        for e in ENGINES:
            if self.ops[e]:
                getattr(block, e)(mk(e))


XRES = 0
CONST = 16384
WCOL = 16416
B2C8 = 16608
ROWS = 17632
BCA = 22752
BCB = 23776
SCR = 24800
FBIG = 26464
F32_COLS = 34688

R_MU = 0
R_SQ = 1024
R_TMP = 2048
R_A = 3072
R_B = 4096
NRM = SCR + 1536

BIDENT = 0
BONEH = 128
BSROW = 1152
BONES = 3200
BMISC = 3328
BROW = 3584
B2B = 5632
BBIG = 6656
BF_COLS = 36864

CONST_VALS = {0: EPS, 1: 1.0 / D, 2: -1.0, 3: -ATT_SCALE, 4: -ATTB_SCALE,
              5: 1.0, 6: 0.5, 7: 1.0 / NE}
C_EPS, C_INVD, C_NEG1, C_NATT, C_NATTB, C_ONE, C_HALF, C_INV8 = range(8)

MCOL = WCOL + 176


def build_nc():
    nc = bass.Bass("TRN2", target_bir_lowering=False)
    p = Prog()

    dI = {}
    for name, shape, dt in [
        ("x0", [D, TF], F32),
        ("wq", [D, D], F32), ("wk", [D, D], F32), ("wv", [D, D], F32),
        ("wp", [D, D], F32), ("bp_cols", [128, 8], F32),
        ("qkvw_t", [D, 3 * D], F32), ("qkvb_qk", [1, 2 * D], F32),
        ("qkvb_v_cols", [128, 8], F32),
        ("projw_t", [D, D], F32), ("projb_cols", [128, 8], F32),
        ("nw_cols", [128, 8], F32), ("nb_cols", [128, 8], F32),
        ("wg", [D, NE], F32),
        ("w1", [NE, D, HID], F32), ("b1_cols", [128, 128], F32),
        ("w2", [NE, HID, D], F32), ("b2r", [NE, D], F32),
        ("ident", [128, 128], BF16), ("onehot", [NE, NE * 128], BF16),
        ("sel2", [16, 1024], BF16),
    ]:
        dI[name] = nc.dram_tensor(name, shape, dt, kind="ExternalInput").ap()
    out = nc.dram_tensor("out", [D, TO], F32, kind="ExternalOutput").ap()

    skind = "ExternalOutput" if DEBUG else "Internal"
    dS = {}
    for name, shape, dt in [
        ("q_sp", [D, TF], BF16), ("k_sp", [D, TF], BF16),
        ("v_sp", [NHP, TF, 130], BF16), ("o_sp", [D, TF], BF16),
        ("x1p", [D, TO], F32), ("x2p", [D, TO], F32),
        ("xln3o", [D, TO], BF16),
        ("qb_sp", [TF, D], BF16), ("kb_sp", [TF, D], BF16),
        ("w1b", [NE, D, HID], BF16), ("w2b", [NE, HID, D], BF16),
        ("vb_sp", [D, TO], BF16), ("hh_sp", [D, TO], BF16),
        ("dn_sp", [16, TF], F32),
        ("dbg", [D, TF], F32), ("dbg2", [128, 64], F32),
    ]:
        dS[name] = nc.dram_tensor(name, shape, dt, kind=skind).ap()

    with contextlib.ExitStack() as stack:
        AFt = stack.enter_context(nc.sbuf_tensor("AF", [128, F32_COLS], F32))
        ABt = stack.enter_context(nc.sbuf_tensor("AB", [128, BF_COLS], BF16))
        PS = stack.enter_context(nc.psum_tensor("PS", [128, 3584], F32))
        PT = stack.enter_context(nc.psum_tensor("PTb", [128, 1024], BF16))

        def af(c0, c1, p0=0, p1=128):
            return AFt[p0:p1, c0:c1]

        def ab(c0, c1, p0=0, p1=128):
            return ABt[p0:p1, c0:c1]

        def bank(i, n=1, p0=0, p1=128):
            return PS[p0:p1, i * 512:(i + n) * 512]

        def ptb(c0, c1, p0=0, p1=128):
            return PT[p0:p1, c0:c1]

        build_phases(nc, p, dI, dS, out, af, ab, bank, ptb)

        with contextlib.ExitStack() as st2:
            p.build(nc, st2)
    return nc


def build_phases(nc, p, dI, dS, out, af, ab, bank, ptb):
    PE, DV, AC, GP, SQ, GQ = "pe", "dv", "ac", "gp", "sq", "gq"
    bank_war = {}

    def bw(i):
        return bank_war.get(i)

    def setbw(i, tok):
        for k in (i if isinstance(i, (list, tuple)) else [i]):
            bank_war[k] = tok

    def rows(off, n, w=1):
        return af(ROWS + off, ROWS + off + n, 0, w)

    def cst(i, np_=128):
        return af(CONST + i, CONST + i + 1, 0, np_)

    def ones_col():
        return ab(BONES, BONES + 1)

    def ones_row(m=128):
        return ab(BONES, BONES + m, 0, 1)

    for idx, val in CONST_VALS.items():
        p.emit("vector", lambda e, i=idx, v=val: e.memset(af(CONST + i, CONST + i + 1), v), inc=DV)
    p.emit("vector", lambda e: e.memset(ab(BONES, BONES + 128), 1.0), inc=DV)
    tz0 = p.emit("vector", lambda e: e.memset(ab(BMISC, BMISC + 2), 0.0), inc=DV)
    tz1 = p.emit("vector", lambda e: e.memset(ab(BMISC, BMISC + 1, 0, 64), 1.0), wait=[tz0], inc=DV)
    p.emit("vector", lambda e: e.memset(ab(BMISC + 1, BMISC + 2, 64, 128), 1.0), wait=[tz1], inc=DV)
    p.emit("vector", lambda e: e.memset(af(NRM, NRM + 64), 0.0), inc=DV)
    p.dma("sync", lambda e: e.dma_start(out=ab(BIDENT, BIDENT + 128), in_=dI["ident"][:, :]), inc=SQ)
    p.dma("sync", lambda e: e.dma_start(out=ab(BONEH, BONEH + 1024, 0, NE), in_=dI["onehot"][:, :]), inc=SQ)
    p.dma("sync", lambda e: e.dma_start(out=ab(BSROW, BSROW + 1024, 0, 16), in_=dI["sel2"][:, :]), inc=SQ)
    for i, nm in enumerate(["bp_cols", "qkvb_v_cols", "projb_cols", "nw_cols", "nb_cols"]):
        p.dma("sync", lambda e, nm=nm, i=i: e.dma_start(
            out=af(WCOL + 8 * i, WCOL + 8 * i + 8), in_=dI[nm][:, :]), inc=SQ)
    p.dma("sync", lambda e: e.dma_start(out=af(WCOL + 48, WCOL + 176), in_=dI["b1_cols"][:, :]), inc=SQ)
    p.dma("gpsimd", lambda e: e.dma_start(out=ab(B2B, B2B + 1024, 0, NE), in_=dI["b2r"][:, :]), inc=GQ)
    p.dma("sync", lambda e: e.dma_start(
        out=af(XRES, XRES + 16384).rearrange("p (d t) -> p d t", d=NCH),
        in_=dI["x0"].rearrange("(d p) t -> p d t", p=128)[:, :, 0:TO]), inc=SQ)
    t_init = p.tok(SQ)
    t_initv = p.tok(DV)

    def ln_sweep(w, src_fn, xln, sqb, wait_src, war_sq=(), war_brow=()):
        nh = w // 512
        sq_mm_tok = [None, None]
        t_cast = []
        t_sum_mm = None
        t_sumsq_mm = None
        for d in range(NCH):
            slot = sqb + (d % 2) * w
            tc = p.emit("vector", lambda e, d=d: e.tensor_copy(
                ab(xln + d * w, xln + (d + 1) * w), src_fn(d)), wait=wait_src, inc=DV)
            t_cast.append(tc)
            wl = list(wait_src) + list(war_sq if d < 2 else ()) + \
                ([sq_mm_tok[d % 2]] if d >= 2 else [])
            ts = p.emit("scalar", lambda e, d=d, slot=slot: e.activation(
                ab(slot, slot + w), src_fn(d), AF.Square), wait=wl, inc=AC)
            for j in range(nh):
                t_sum_mm = p.emit("tensor", lambda e, d=d, j=j: e.matmul(
                    bank(2, 2, 0, 1)[:, j * 512:(j + 1) * 512], ones_col(),
                    ab(xln + d * w + j * 512, xln + d * w + (j + 1) * 512),
                    start=(d == 0), stop=(d == NCH - 1)),
                    wait=[tc, t_initv] + ([bw(2)] if d == 0 and j == 0 else []),
                    inc=(PE if d == NCH - 1 and j == nh - 1 else None))
            last = None
            for j in range(nh):
                last = p.emit("tensor", lambda e, d=d, j=j, slot=slot: e.matmul(
                    bank(4, 2, 0, 1)[:, j * 512:(j + 1) * 512], ones_col(),
                    ab(slot + j * 512, slot + (j + 1) * 512),
                    start=(d == 0), stop=(d == NCH - 1)),
                    wait=[ts] + ([bw(4)] if d == 0 and j == 0 else []),
                    inc=(PE if j == nh - 1 else None))
            sq_mm_tok[d % 2] = last
            t_sumsq_mm = last
        t1 = p.emit("vector", lambda e: e.tensor_scalar(
            rows(R_MU, w), bank(2, 2, 0, 1)[:, 0:w], cst(C_INVD, 1), None, op0=ALU.mult),
            wait=[t_sum_mm], inc=DV)
        t2 = p.emit("vector", lambda e: e.tensor_scalar(
            rows(R_SQ, w), bank(4, 2, 0, 1)[:, 0:w], cst(C_INVD, 1), None, op0=ALU.mult),
            wait=[t_sumsq_mm], inc=DV)
        setbw([2, 3], t1)
        setbw([4, 5], t2)
        t3 = p.emit("vector", lambda e: e.tensor_mul(rows(R_TMP, w), rows(R_MU, w), rows(R_MU, w)),
                    wait=[t1], inc=DV)
        t4 = p.emit("vector", lambda e: e.tensor_sub(rows(R_A, w), rows(R_SQ, w), rows(R_TMP, w)),
                    wait=[t2, t3], inc=DV)
        t5 = p.emit("scalar", lambda e: e.activation(rows(R_B, w), rows(R_A, w), AF.Ln, bias=cst(C_EPS, 1)),
                    wait=[t4], inc=AC)
        t6 = p.emit("scalar", lambda e: e.activation(rows(R_A, w), rows(R_B, w), AF.Exp, scale=-0.5),
                    wait=[t5], inc=AC)
        t6b = p.emit("vector", lambda e: e.tensor_scalar(
            rows(R_TMP, w), rows(R_A, w), cst(C_NEG1, 1), None, op0=ALU.mult), wait=[t6], inc=DV)
        t7 = p.emit("vector", lambda e: e.tensor_mul(rows(R_B, w), rows(R_MU, w), rows(R_TMP, w)),
                    wait=[t6b], inc=DV)
        t_bc = []
        for bi, (roff, dst, tr) in enumerate(((R_A, BCA, t6), (R_B, BCB, t7))):
            tcast = p.emit("vector", lambda e, roff=roff, bi=bi: e.tensor_copy(
                ab(BROW + bi * 1024, BROW + bi * 1024 + w, 0, 1), rows(roff, w)),
                wait=[tr] + list(war_brow), inc=DV)
            for j in range(nh):
                tm = p.emit("tensor", lambda e, bi=bi, j=j: e.matmul(
                    bank(6), ones_row(),
                    ab(BROW + bi * 1024 + j * 512, BROW + bi * 1024 + (j + 1) * 512, 0, 1),
                    start=True, stop=True), wait=[tcast, bw(6)], inc=PE)
                tv = p.emit("vector", lambda e, dst=dst, j=j: e.tensor_copy(
                    af(dst + j * 512, dst + (j + 1) * 512), bank(6)), wait=[tm], inc=DV)
                setbw(6, tv)
                t_bc.append(tv)
        t_apply = []
        for d in range(NCH):
            tm = p.emit("vector", lambda e, d=d: e.tensor_mul(
                af(SCR, SCR + w), src_fn(d), af(BCA, BCA + w)), wait=t_bc, inc=DV)
            t_apply.append(p.emit("vector", lambda e, d=d: e.tensor_add(
                ab(xln + d * w, xln + (d + 1) * w), af(SCR, SCR + w), af(BCB, BCB + w)),
                wait=[tm], inc=DV))
        return t_apply

    if N_PH < 1:
        finish(p, out, af)
        return

    XLN = BBIG
    SQB = XLN + 8192
    STG = SQB
    NSQ = SQB + 2048
    WQS = NSQ + 1024
    WKS = WQS + 2048
    WVR = WKS + 2048
    VST = WVR + 8192
    assert VST + 1040 <= BF_COLS, VST + 1040

    p.dma("gpsimd", lambda e: e.dma_start(
        out=ab(WVR, WVR + 8192).rearrange("p (d m) -> p d m", d=NCH),
        in_=dI["wv"].rearrange("(d p) m -> p d m", p=128)), inc=GQ)
    t_wv = p.tok(GQ)

    PCW = VST + 1040 + 16
    assert PCW + 4096 <= BF_COLS
    pcw_chunks = []
    for ei in range(NE):
        for srcw, dstw, a in ((dI["w1"], dS["w1b"], NCH), (dI["w2"], dS["w2b"], 16)):
            for pc in range(8):
                pcw_chunks.append((srcw, dstw, ei, a, pc))
    pcw_state = {"i": 0, "war": [None, None], "last": None}

    def emit_precast(n):
        for _ in range(n):
            i = pcw_state["i"]
            if i >= len(pcw_chunks):
                return
            pcw_state["i"] = i + 1
            srcw, dstw, ei, a, pc = pcw_chunks[i]
            sl2 = i % 2
            c = a // 8
            tl = p.dma("gpsimd", lambda e, srcw=srcw, ei=ei, pc=pc, sl2=sl2, c=c: e.dma_start(
                out=ab(PCW + sl2 * 2048, PCW + (sl2 + 1) * 2048).rearrange("p (c i) -> p c i", c=c),
                in_=srcw[ei].rearrange("(c p) i -> p c i", p=128)[:, pc * c:(pc + 1) * c, :]),
                wait=[pcw_state["war"][sl2]], inc=GQ)
            ts = p.dma("sync", lambda e, dstw=dstw, ei=ei, pc=pc, sl2=sl2, c=c: e.dma_start(
                out=dstw[ei].rearrange("(c p) i -> p c i", p=128)[:, pc * c:(pc + 1) * c, :],
                in_=ab(PCW + sl2 * 2048, PCW + (sl2 + 1) * 2048).rearrange("p (c i) -> p c i", c=c)),
                wait=[tl], inc=SQ)
            pcw_state["war"][sl2] = ts
            pcw_state["last"] = ts
    t_vones = p.emit("vector", lambda e: e.memset(
        ab(VST, VST + 1040).rearrange("p (h f) -> p h f", h=16)[:, :, 64:65], 1.0), inc=DV)

    wq_tok = [None, None]; wk_tok = [None, None]
    wq_war = [None, None]; wk_war = [None, None]
    stg_war = [None, None]
    dbg_war = []
    vst_war = t_vones
    xln_war = None
    sq_region_war = []

    w = 1024
    for s in range(4):
        own = s < 2
        if own:
            def src_fn(d, s=s):
                return af(XRES + d * 2048 + (s % 2) * w, XRES + d * 2048 + (s % 2) * w + w)
            t_src = [t_init]
        else:
            tl = p.dma("sync", lambda e, s=s: e.dma_start(
                out=af(FBIG, FBIG + 8192).rearrange("p (d t) -> p d t", d=NCH),
                in_=dI["x0"].rearrange("(d p) t -> p d t", p=128)[:, :, TO + (s - 2) * w: TO + (s - 1) * w]),
                wait=[p.tok(DV), p.tok(AC)], inc=SQ)

            def src_fn(d):
                return af(FBIG + d * w, FBIG + (d + 1) * w)
            t_src = [tl]
        if xln_war is not None:
            t_src = t_src + [xln_war] + list(dbg_war)

        t_apply = ln_sweep(w, src_fn, XLN, SQB, t_src, war_sq=sq_region_war)
        sq_region_war = []
        if DEBUG:
            tdbg = p.dma("gpsimd", lambda e, s=s: e.dma_start(
                out=dS["dbg"].rearrange("(d p) t -> p d t", p=128)[:, :, s * w:(s + 1) * w],
                in_=ab(XLN, XLN + 8192).rearrange("p (d t) -> p d t", d=NCH)),
                wait=[t_apply[-1]], inc=GQ)
            sq_region_war.append(tdbg)
            dbg_war = [tdbg]

        last_xln_mm = None
        for wi, (wdram, slots, stoks, swars, spill) in enumerate((
                (dI["wq"], WQS, wq_tok, wq_war, dS["q_sp"]),
                (dI["wk"], WKS, wk_tok, wk_war, dS["k_sp"]))):
            for hp in range(NHP):
                sl = hp % 2
                if True:
                    tw = p.dma("gpsimd", lambda e, hp=hp, sl=sl, wdram=wdram, slots=slots: e.dma_start(
                        out=ab(slots + sl * 1024, slots + (sl + 1) * 1024).rearrange(
                            "p (d m) -> p d m", d=NCH),
                        in_=wdram.rearrange("(d p) m -> p d m", p=128)[:, :, hp * 128:(hp + 1) * 128]),
                        wait=[swars[sl]], inc=GQ)
                    stoks[sl] = tw
                last = None
                for d in range(NCH):
                    for j in range(2):
                        last = p.emit("tensor", lambda e, d=d, j=j, sl=sl, slots=slots: e.matmul(
                            bank(0, 2)[:, j * 512:(j + 1) * 512],
                            ab(slots + sl * 1024 + d * 128, slots + sl * 1024 + (d + 1) * 128),
                            ab(XLN + d * w + j * 512, XLN + d * w + (j + 1) * 512),
                            start=(d == 0), stop=(d == NCH - 1)),
                            wait=[t_apply[d], stoks[sl]] + ([bw(0)] if d == 0 and j == 0 else []),
                            inc=(PE if d == NCH - 1 and j == 1 else None))
                t_mm = last
                swars[sl] = t_mm
                last_xln_mm = t_mm
                ssl = hp % 2
                tv = p.emit("vector", lambda e, ssl=ssl: e.tensor_copy(
                    ab(STG + ssl * 1024, STG + (ssl + 1) * 1024), bank(0, 2)),
                    wait=[t_mm, stg_war[ssl]], inc=DV)
                setbw([0, 1], tv)
                td = p.dma("sync", lambda e, hp=hp, s=s, ssl=ssl, spill=spill: e.dma_start(
                    out=spill[hp * 128:(hp + 1) * 128, s * w:(s + 1) * w],
                    in_=ab(STG + ssl * 1024, STG + (ssl + 1) * 1024)), wait=[tv], inc=SQ)
                stg_war[ssl] = td
                sq_region_war.append(td)
                tqs = p.emit("scalar", lambda e, ssl=ssl: e.activation(
                    ab(NSQ, NSQ + 1024), ab(STG + ssl * 1024, STG + (ssl + 1) * 1024), AF.Square),
                    wait=[tv, (bank_war.get("nsq"))], inc=AC)
                bank_war["nsq"] = None
                for head in range(2):
                    lastn = None
                    for j in range(2):
                        lastn = p.emit("tensor", lambda e, head=head, j=j: e.matmul(
                            bank(2, 2, 32 * head, 32 * head + 1)[:, j * 512:(j + 1) * 512],
                            ab(BMISC + head, BMISC + head + 1),
                            ab(NSQ + j * 512, NSQ + (j + 1) * 512),
                            start=True, stop=True),
                            wait=[tqs] + ([bw(2)] if j == 0 else []),
                            inc=(PE if j == 1 else None))
                    nidx = NRM + wi * 16 + hp * 2 + head
                    tr1 = p.emit("vector", lambda e, head=head: e.reduce_max(
                        af(SCR + 1024, SCR + 1025, 0, 1),
                        bank(2, 2, 32 * head, 32 * head + 1)[:, 0:1024], axis=AX.X),
                        wait=[lastn], inc=DV)
                    tr2 = p.emit("vector", lambda e, nidx=nidx: e.tensor_max(
                        af(nidx, nidx + 1, 0, 1), af(nidx, nidx + 1, 0, 1),
                        af(SCR + 1024, SCR + 1025, 0, 1)), wait=[tr1, t_initv], inc=DV)
                    setbw([2, 3], tr2)
                bank_war["nsq"] = p.tok(PE)
                emit_precast(2)

        for tt in range(8):
            last = None
            for d in range(NCH):
                for j in range(2):
                    last = p.emit("tensor", lambda e, tt=tt, d=d, j=j: e.matmul(
                        bank(0, 2)[:, j * 512:(j + 1) * 512],
                        ab(XLN + d * w + tt * 128, XLN + d * w + tt * 128 + 128),
                        ab(WVR + d * 1024 + j * 512, WVR + d * 1024 + (j + 1) * 512),
                        start=(d == 0), stop=(d == NCH - 1)),
                        wait=[t_apply[d], t_wv] + ([bw(0)] if d == 0 and j == 0 else []),
                        inc=(PE if d == NCH - 1 and j == 1 else None))
            t_mm = last
            last_xln_mm = t_mm
            tv = p.emit("vector", lambda e: e.tensor_copy(
                ab(VST, VST + 1040).rearrange("p (a b c) -> p a b c", a=8, c=65)[:, :, :, 0:64],
                bank(0, 2).rearrange("p (a b c) -> p a b c", a=8, c=64)),
                wait=[t_mm, vst_war], inc=DV)
            setbw([0, 1], tv)
            td = p.dma("sync", lambda e, s=s, tt=tt: e.dma_start(
                out=dS["v_sp"][:, s * w + tt * 128: s * w + tt * 128 + 128, :].rearrange("a r f -> r a f"),
                in_=ab(VST, VST + 1040).rearrange("p (a f) -> p a f", a=8)),
                wait=[tv], inc=SQ)
            vst_war = td
            sq_region_war.append(td)
        xln_war = last_xln_mm

    for h in range(16):
        hp, head = h // 2, h % 2
        qn = af(NRM + hp * 2 + head, NRM + hp * 2 + head + 1, 0, 1)
        kn = af(NRM + 16 + hp * 2 + head, NRM + 16 + hp * 2 + head + 1, 0, 1)
        sc = af(SCR + 1024 + 1 + h, SCR + 1024 + 2 + h, 0, 1)
        t1 = p.emit("vector", lambda e, qn=qn, kn=kn, sc=sc: e.tensor_mul(sc, qn, kn), inc=DV)
        t2 = p.emit("scalar", lambda e, sc=sc: e.activation(sc, sc, AF.Ln, bias=cst(C_EPS, 1)),
                    wait=[t1], inc=AC)
        t3 = p.emit("scalar", lambda e, sc=sc: e.activation(sc, sc, AF.Exp, scale=0.5),
                    wait=[t2], inc=AC)
        t3b = p.emit("vector", lambda e, sc=sc, h=h: e.tensor_copy(
            ab(BROW + h, BROW + h + 1, 0, 1), sc), wait=[t3], inc=DV)
        tm = p.emit("tensor", lambda e, h=h: e.matmul(
            bank(6)[:, 0:1], ones_row(), ab(BROW + h, BROW + h + 1, 0, 1),
            start=True, stop=True), wait=[t3b, bw(6)], inc=PE)
        tv = p.emit("vector", lambda e, h=h: e.tensor_scalar(
            af(MCOL + h, MCOL + h + 1), bank(6)[:, 0:1], cst(C_NATT), None, op0=ALU.mult),
            wait=[tm], inc=DV)
        setbw(6, tv)
    if DEBUG:
        p.dma("sync", lambda e: e.dma_start(
            out=dS["dbg2"][:, 0:16], in_=af(MCOL, MCOL + 16)), wait=[p.tok(DV)], inc=SQ)
    t_mcol = p.tok(DV)

    assert pcw_state["i"] == len(pcw_chunks), pcw_state["i"]
    t_precast = pcw_state["last"]

    def ffn_phase(sweeps, srcp, dstp):
        XLN5 = BBIG
        SQB5 = BBIG + 8192
        HPR = BBIG + 10240
        W1S1 = BBIG + 18432
        W2B = BBIG + 20480
        SBC = BBIG + 28672
        HTM = BBIG + 29696
        WGS = BMISC + 8
        t_wg = p.dma("gpsimd", lambda e: e.dma_start(
            out=ab(WGS, WGS + 64).rearrange("p (d m) -> p d m", d=NCH),
            in_=dI["wg"].rearrange("(d p) m -> p d m", p=128)),
            wait=[p.tok(PE)], inc=GQ)
        w1_tok = [None, None]; w1_war = [None, None]
        w2_tok = [None, None, None, None]; w2_war = [None, None, None, None]
        sbc_war = [None, None]
        htm_war = [None]
        hpr_tok = [None] * 16
        w2done_prev = None

        for s5i, (kind, idx) in enumerate(sweeps):
            own = kind == "own"
            w = 1024
            if own:
                def src_fn(d, idx=idx):
                    return af(XRES + d * 2048 + idx * w, XRES + d * 2048 + idx * w + w)
                t_src = [p.tok(PE), p.tok(DV)]
            else:
                tl = p.dma("sync", lambda e, idx=idx, srcp=srcp: e.dma_start(
                    out=af(FBIG, FBIG + 8192).rearrange("p (d t) -> p d t", d=NCH),
                    in_=srcp.rearrange("(d p) t -> p d t", p=128)[:, :, idx * w:(idx + 1) * w]),
                    wait=[p.tok(DV), p.tok(AC)], inc=SQ)

                def src_fn(d):
                    return af(FBIG + d * w, FBIG + (d + 1) * w)
                t_src = [tl]
            t_apply = ln_sweep(w, src_fn, XLN5, SQB5, t_src,
                               war_sq=[w1_war[0], p.tok(DV)],
                               war_brow=[bank_war.get("brow")])
            t_xln_ready = t_apply

            for tt in range(8):
                lastg = None
                for d in range(NCH):
                    lastg = p.emit("tensor", lambda e, tt=tt, d=d: e.matmul(
                        bank(0)[:, 0:8],
                        ab(XLN5 + d * w + tt * 128, XLN5 + d * w + tt * 128 + 128),
                        ab(WGS + d * 8, WGS + (d + 1) * 8, 0, 128),
                        start=(d == 0), stop=(d == NCH - 1)),
                        wait=[t_apply[d], t_wg] + ([bw(0)] if d == 0 else []),
                        inc=(PE if d == NCH - 1 else None))
                g = SCR
                tmx = p.emit("vector", lambda e: e.reduce_max(
                    af(g + 16, g + 17), bank(0)[:, 0:8], axis=AX.X), wait=[lastg], inc=DV)
                tng = p.emit("vector", lambda e: e.tensor_scalar(
                    af(g + 17, g + 18), af(g + 16, g + 17), cst(C_NEG1), None, op0=ALU.mult),
                    wait=[tmx], inc=DV)
                tex = p.emit("scalar", lambda e: e.activation(
                    af(g, g + 8), bank(0)[:, 0:8], AF.Exp, bias=af(g + 17, g + 18)),
                    wait=[tng], inc=AC)
                setbw(0, tex)
                tm1 = p.emit("vector", lambda e: e.reduce_max(
                    af(g + 16, g + 17), af(g, g + 8), axis=AX.X), wait=[tex], inc=DV)
                teq = p.emit("vector", lambda e: e.tensor_scalar(
                    af(g + 8, g + 16), af(g, g + 8), af(g + 16, g + 17), None, op0=ALU.is_ge),
                    wait=[tm1], inc=DV)
                tms = p.emit("vector", lambda e: e.tensor_sub(
                    af(g + 8, g + 16), af(g, g + 8), af(g + 8, g + 16)), wait=[teq], inc=DV)
                tm2 = p.emit("vector", lambda e: e.reduce_max(
                    af(g + 18, g + 19), af(g + 8, g + 16), axis=AX.X), wait=[tms], inc=DV)
                tmk = p.emit("vector", lambda e: e.tensor_scalar(
                    af(g + 8, g + 16), af(g, g + 8), af(g + 18, g + 19), None, op0=ALU.is_ge),
                    wait=[tm2], inc=DV)
                tdn = p.emit("vector", lambda e: e.tensor_scalar(
                    af(g + 18, g + 19), af(g + 18, g + 19), cst(C_ONE), None, op0=ALU.add),
                    wait=[tmk], inc=DV)
                trc = p.emit("vector", lambda e: e.reciprocal(
                    af(g + 19, g + 20), af(g + 18, g + 19)), wait=[tdn], inc=DV)
                tsc = p.emit("vector", lambda e: e.tensor_mul(
                    af(g, g + 8), af(g, g + 8), af(g + 8, g + 16)), wait=[tmk], inc=DV)
                tsc2 = p.emit("vector", lambda e: e.tensor_scalar(
                    af(g, g + 8), af(g, g + 8), af(g + 19, g + 20), None, op0=ALU.mult),
                    wait=[trc, tsc], inc=DV)
                tbf = p.emit("vector", lambda e: e.tensor_copy(
                    ab(BROW, BROW + 8), af(g, g + 8)), wait=[tsc2], inc=DV)
                ttr = p.emit("tensor", lambda e: e.transpose(
                    ptb(0, 128, 0, 8), ab(BROW, BROW + 8), ab(BIDENT, BIDENT + 128)),
                    wait=[tbf, bw("pt")], inc=PE)
                tev = p.emit("vector", lambda e, tt=tt: e.tensor_copy(
                    ab(BSROW + tt * 128, BSROW + (tt + 1) * 128, 0, 8), ptb(0, 128, 0, 8)),
                    wait=[ttr], inc=DV)
                bank_war["pt"] = tev
                bank_war["brow"] = ttr

            if FFN_GATE_ONLY:
                continue
            for th in range(2):
                tcol = idx * w + th * 512 if own else None
                for ei in range(FFN_NE):
                    esl = ei % 2
                    tsb = p.emit("tensor", lambda e, ei=ei, th=th: e.matmul(
                        bank(6), ab(BONEH + ei * 128, BONEH + (ei + 1) * 128, 0, 8),
                        ab(BSROW + th * 512, BSROW + (th + 1) * 512, 0, 8),
                        start=True, stop=True), wait=[p.tok(DV), bw(6)], inc=PE)
                    tsv = p.emit("vector", lambda e, esl=esl: e.tensor_copy(
                        ab(SBC + esl * 512, SBC + (esl + 1) * 512), bank(6)),
                        wait=[tsb, sbc_war[esl]], inc=DV)
                    setbw(6, tsv)
                    for hc4 in range(8):
                        wsl = hc4 % 2
                        woff = SQB5 if wsl == 0 else W1S1
                        wl = [w1_war[wsl]]
                        if s5i == 0 and th == 0 and ei == 0 and hc4 < 2:
                            wl.append(t_precast)
                        if th == 0 and ei == 0 and hc4 == 0:
                            wl.append(t_apply[-1])
                        w1_tok[wsl] = p.dma("sync", lambda e, ei=ei, hc4=hc4, woff=woff: e.dma_start(
                            out=ab(woff, woff + 2048).rearrange("p (c i) -> p c i", c=NCH),
                            in_=dS["w1b"][ei].rearrange("(c p) i -> p c i", p=128)[:, :, hc4 * 256:(hc4 + 1) * 256]),
                            wait=wl, inc=SQ)
                        for hc2 in range(2):
                            hc = hc4 * 2 + hc2
                            hb = hc % 3
                            last = None
                            for d in range(NCH):
                                last = p.emit("tensor", lambda e, d=d, hc=hc, hc2=hc2, th=th, woff=woff: e.matmul(
                                    bank(hc % 3),
                                    ab(woff + d * 256 + hc2 * 128, woff + d * 256 + hc2 * 128 + 128),
                                    ab(XLN5 + d * w + th * 512, XLN5 + d * w + (th + 1) * 512),
                                    start=(d == 0), stop=(d == NCH - 1)),
                                    wait=[w1_tok[wsl], t_xln_ready[d]] + ([bw(hc % 3)] if d == 0 else []),
                                    inc=(PE if d == NCH - 1 else None))
                            w1_war[wsl] = last
                            tre = p.emit("scalar", lambda e, hc=hc, ei=ei, hb=hc % 3: e.activation(
                                ab(HTM, HTM + 512), bank(hb), AF.Relu,
                                bias=af(WCOL + 48 + ei * 16 + hc, WCOL + 48 + ei * 16 + hc + 1)),
                                wait=[last, htm_war[0]], inc=AC)
                            setbw(hc % 3, tre)
                            tmu = p.emit("vector", lambda e, hc=hc, esl=esl: e.tensor_mul(
                                ab(HPR + hc * 512, HPR + (hc + 1) * 512),
                                ab(HTM, HTM + 512), ab(SBC + esl * 512, SBC + (esl + 1) * 512)),
                                wait=[tre, tsv, w2done_prev], inc=DV)
                            htm_war[0] = tmu
                            hpr_tok[hc] = tmu
                    if DEBUG and s5i == 0 and th == 0 and ei == 0:
                        p.dma("gpsimd", lambda e: e.dma_start(
                            out=dS["dbg2"][0:8, 0:64].rearrange("a b -> a b"),
                            in_=ab(BSROW, BSROW + 64, 0, 8)), wait=[p.tok(DV)], inc=GQ)
                        p.dma("gpsimd", lambda e: e.dma_start(
                            out=dS["dbg"].rearrange("(c p) t -> p c t", p=128)[:, 0:2, :],
                            in_=ab(HPR, HPR + 8192).rearrange("p (c t) -> p c t", c=2)),
                            wait=[p.tok(DV)], inc=GQ)
                    for half in range(2):
                        for c in range(4):
                            wl2 = [w2_war[c]]
                            if s5i == 0 and th == 0 and ei == 0 and half == 0:
                                wl2.append(t_precast)
                            w2_tok[c] = p.dma("gpsimd", lambda e, ei=ei, half=half, c=c: e.dma_start(
                                out=ab(W2B + c * 2048, W2B + (c + 1) * 2048).rearrange("p (c2 i) -> p c2 i", c2=2),
                                in_=dS["w2b"][ei].rearrange("(c2 p) i -> p c2 i", p=128)[:, half * 8 + c * 2:half * 8 + c * 2 + 2, :]),
                                wait=wl2, inc=GQ)
                        for do in range(NCH):
                            yb = 4 + do % 2
                            first = (ei == 0 and half == 0)
                            tprev = None
                            if first:
                                tprev = p.emit("tensor", lambda e, do=do, th=th: e.matmul(
                                    bank(4 + do % 2),
                                    ab(B2B + do * 128, B2B + (do + 1) * 128, 0, 8),
                                    ab(BSROW + th * 512, BSROW + (th + 1) * 512, 0, 8),
                                    start=True, stop=False),
                                    wait=[p.tok(DV), bw(4 + do % 2)])
                            last = None
                            for hcq in range(8):
                                hc = half * 8 + hcq
                                csl = hcq // 2
                                give_tok = hcq == 7 or (do == NCH - 1 and hcq % 2 == 1)
                                last = p.emit("tensor", lambda e, do=do, hc=hc, hcq=hcq, csl=csl, first=first: e.matmul(
                                    bank(4 + do % 2),
                                    ab(W2B + csl * 2048 + (hcq % 2) * 1024 + do * 128,
                                       W2B + csl * 2048 + (hcq % 2) * 1024 + do * 128 + 128),
                                    ab(HPR + hc * 512, HPR + (hc + 1) * 512),
                                    start=(hcq == 0 and not first), stop=(hcq == 7)),
                                    wait=[w2_tok[csl], hpr_tok[hc]] +
                                         ([bw(4 + do % 2)] if hcq == 0 and not first else []),
                                    inc=(PE if give_tok else None))
                                if do == NCH - 1 and hcq % 2 == 1:
                                    w2_war[hcq // 2] = last
                            t_mm = last
                            if own:
                                dst = af(XRES + do * 2048 + tcol, XRES + do * 2048 + tcol + 512)
                            else:
                                dst = af(FBIG + do * w + th * 512, FBIG + do * w + (th + 1) * 512)
                            ta = p.emit("vector", lambda e, dst=dst, do=do: e.tensor_add(
                                dst, bank(4 + do % 2), dst), wait=[t_mm], inc=DV)
                            setbw(4 + do % 2, ta)
                    w2done_prev = p.tok(PE)
            if not own:
                tst = p.dma("sync", lambda e, idx=idx, dstp=dstp: e.dma_start(
                    out=dstp.rearrange("(d p) t -> p d t", p=128)[:, :, idx * w:(idx + 1) * w],
                    in_=af(FBIG, FBIG + 8192).rearrange("p (d t) -> p d t", d=NCH)),
                    wait=[p.tok(DV)], inc=SQ)
        return

    if N_PH < 3:
        finish(p, out, af)
        return

    SLOT = 12352
    P3K = BBIG
    P3Q = BBIG + 4096
    P3V = BBIG + 8192
    PTBUF = BBIG + 2 * SLOT
    OST = PTBUF + 4096
    assert OST + 1024 <= BF_COLS

    t_spill_done = p.tok(SQ)
    hp_load_tok = [None, None]
    hp_war = [None, None]

    def load_hp(hp):
        sl = hp % 2
        wl = [t_spill_done, hp_war[sl]]
        t1 = p.dma("sync", lambda e, hp=hp, sl=sl: e.dma_start(
            out=ab(P3K + sl * SLOT, P3K + sl * SLOT + 4096),
            in_=dS["k_sp"][hp * 128:(hp + 1) * 128, :]), wait=wl, inc=SQ)
        t2 = p.dma("sync", lambda e, hp=hp, sl=sl: e.dma_start(
            out=ab(P3Q + sl * SLOT, P3Q + sl * SLOT + 4096),
            in_=dS["q_sp"][hp * 128:(hp + 1) * 128, :]), inc=SQ)
        t3 = p.dma("sync", lambda e, hp=hp, sl=sl: e.dma_start(
            out=ab(P3V + sl * SLOT, P3V + sl * SLOT + 4160).rearrange("p (c f) -> p c f", c=32),
            in_=dS["v_sp"][hp].rearrange("(c p) f -> p c f", p=128)), inc=SQ)
        hp_load_tok[sl] = t3
        return t3

    load_hp(0)
    ost_war = [None, None]
    dn_war = [None, None]
    for hp in range(NHP):
        sl = hp % 2
        t_ld = hp_load_tok[sl]
        if hp + 1 < NHP:
            load_hp(hp + 1)
        last_pe_of_hp = None
        for q5 in range(8):
            t_pv = [None, None]
            texp_tok = [[None, None], [None, None]]
            tpv_tok = [[None, None], [None, None]]

            def emit_S_pair(pair, sl=sl, q5=q5, t_ld=t_ld):
                last = [None, None]
                for j in range(2):
                    for hd in range(2):
                        k = 2 * pair + j
                        sb = hd * 2 + j
                        lkb = ab(P3K + sl * SLOT + k * 128, P3K + sl * SLOT + (k + 1) * 128,
                                 64 * hd, 64 * hd + 64)
                        lqb = ab(P3Q + sl * SLOT + q5 * 512, P3Q + sl * SLOT + (q5 + 1) * 512,
                                 64 * hd, 64 * hd + 64)
                        last[hd] = p.emit("tensor", lambda e, lkb=lkb, lqb=lqb, hd=hd, sb=sb: e.matmul(
                            bank(sb), lkb, lqb,
                            start=True, stop=True, tile_position=(64 * hd, 0)),
                            wait=[t_ld, bw(sb)], inc=(PE if j == 1 else None))
                return last

            def emit_exp(pair, hd, t_s, hp=hp):
                pslot = (pair % 2) * 2 + hd
                texp = p.emit("scalar", lambda e, pslot=pslot, hd=hd, hp=hp: e.activation(
                    ab(PTBUF + pslot * 1024, PTBUF + (pslot + 1) * 1024),
                    bank(hd * 2, 2), AF.Exp, scale=ATT_SCALE,
                    bias=af(MCOL + 2 * hp + hd, MCOL + 2 * hp + hd + 1)),
                    wait=[t_s, t_mcol, tpv_tok[pair % 2][hd]], inc=AC)
                setbw([hd * 2, hd * 2 + 1], texp)
                texp_tok[pair % 2][hd] = texp

            def emit_PV(pair, hd, sl=sl):
                pslot = (pair % 2) * 2 + hd
                last = None
                for j in range(2):
                    k = 2 * pair + j
                    lvb = ab(P3V + sl * SLOT + k * 130 + hd * 65,
                             P3V + sl * SLOT + k * 130 + hd * 65 + 65)
                    last = p.emit("tensor", lambda e, lvb=lvb, k=k, pslot=pslot, j=j, hd=hd: e.matmul(
                        bank(4 + hd, 1, 0, 65), lvb,
                        ab(PTBUF + pslot * 1024 + j * 512, PTBUF + pslot * 1024 + (j + 1) * 512),
                        start=(k == 0), stop=(k == 31)),
                        wait=[texp_tok[pair % 2][hd]] + ([bw(4 + hd)] if k == 0 else []),
                        inc=(PE if j == 1 else None))
                tpv_tok[pair % 2][hd] = last
                if pair == 15:
                    t_pv[hd] = last

            for pair in range(16):
                for hd in range(2):
                    if hd == 0:
                        t_s_pair = emit_S_pair(pair)
                    emit_exp(pair, hd, t_s_pair[hd])
                if pair >= 1:
                    for hd in range(2):
                        emit_PV(pair - 1, hd)
            for hd in range(2):
                emit_PV(15, hd)
            osl = q5 % 2
            tms = []
            for hd in range(2):
                tdn = p.emit("vector", lambda e, hd=hd: e.tensor_copy(
                    af(SCR + hd * 512, SCR + hd * 512 + 512, 0, 1),
                    bank(4 + hd, 1, 64, 65)), wait=[t_pv[hd], dn_war[hd]], inc=DV)
                dn_war[hd] = p.dma("sync", lambda e, hd=hd, hp=hp, q5=q5: e.dma_start(
                    out=dS["dn_sp"][2 * hp + hd:2 * hp + hd + 1, q5 * 512:(q5 + 1) * 512],
                    in_=af(SCR + hd * 512, SCR + hd * 512 + 512, 0, 1)),
                    wait=[tdn], inc=SQ)
                tcp = p.emit("vector", lambda e, hd=hd, osl=osl: e.tensor_copy(
                    ab(OST + osl * 512, OST + (osl + 1) * 512, 64 * hd, 64 * hd + 64),
                    bank(4 + hd, 1, 0, 64)), wait=[t_pv[hd], ost_war[osl]], inc=DV)
                setbw(4 + hd, tcp)
                tms.append(tcp)
            tsp = p.dma("sync", lambda e, hp=hp, q5=q5, osl=osl: e.dma_start(
                out=dS["o_sp"][hp * 128:(hp + 1) * 128, q5 * 512:(q5 + 1) * 512],
                in_=ab(OST + osl * 512, OST + (osl + 1) * 512)), wait=tms, inc=SQ)
            ost_war[osl] = tsp
            last_pe_of_hp = p.tok(PE)
        hp_war[sl] = last_pe_of_hp

    if N_PH < 4:
        finish(p, out, af)
        return

    WPR = BBIG
    OBUF = BBIG + 8192
    assert OBUF + 16384 <= BF_COLS
    t_p3pe = p.tok(PE)
    t_p3dv = p.tok(DV)
    twp = p.dma("gpsimd", lambda e: e.dma_start(
        out=ab(WPR, WPR + 8192).rearrange("p (f m) -> p f m", f=NCH),
        in_=dI["wp"].rearrange("(f p) m -> p f m", p=128)), wait=[t_p3pe], inc=GQ)
    ob_tok = [None, None]
    ob_war = [None, None]

    def load_ob(sw):
        sl = sw % 2
        ob_tok[sl] = p.dma("sync", lambda e, sw=sw, sl=sl: e.dma_start(
            out=ab(OBUF + sl * 8192, OBUF + (sl + 1) * 8192).rearrange("p (f t) -> p f t", f=NCH),
            in_=dS["o_sp"].rearrange("(f p) t -> p f t", p=128)[:, :, sw * 1024:(sw + 1) * 1024]),
            wait=[ob_war[sl], p.tok(SQ) if sw == 0 else None], inc=SQ)

    load_ob(0)
    SEL2 = 1152
    rn_war = None
    rb_war = []
    for sw in range(4):
        own = sw < 2
        sl = sw % 2
        if sw + 1 < 4:
            load_ob(sw + 1)
        if not own:
            tpx = p.dma("sync", lambda e, sw=sw: e.dma_start(
                out=af(FBIG, FBIG + 8192).rearrange("p (d t) -> p d t", d=NCH),
                in_=dI["x0"].rearrange("(d p) t -> p d t", p=128)[:, :, TO + (sw - 2) * 1024: TO + (sw - 1) * 1024]),
                wait=[p.tok(DV)], inc=SQ)
        tdnl = p.dma("sync", lambda e, sw=sw: e.dma_start(
            out=af(SCR, SCR + 1024, 0, 16), in_=dS["dn_sp"][:, sw * 1024:(sw + 1) * 1024]),
            wait=[rn_war], inc=SQ)
        tdr = p.emit("vector", lambda e: e.reciprocal(
            af(SCR, SCR + 1024, 0, 16), af(SCR, SCR + 1024, 0, 16)), wait=[tdnl], inc=DV)
        tdc = p.emit("vector", lambda e: e.tensor_copy(
            ab(BROW, BROW + 1024, 0, 16), af(SCR, SCR + 1024, 0, 16)),
            wait=[tdr] + rb_war, inc=DV)
        rn_war = tdc
        rb_war = []
        rmul_tok = {}
        for f in range(NCH):
            for j in range(2):
                rb = 4 + ((2 * f + j) % 3)
                tm = p.emit("tensor", lambda e, f=f, j=j, rb=rb: e.matmul(
                    bank(rb), ab(SEL2 + f * 128, SEL2 + (f + 1) * 128, 0, 16),
                    ab(BROW + j * 512, BROW + (j + 1) * 512, 0, 16),
                    start=True, stop=True), wait=[tdc, bw(rb)], inc=PE)
                rb_war.append(tm)
                tmu = p.emit("vector", lambda e, f=f, j=j, rb=rb, sl=sl: e.tensor_mul(
                    ab(OBUF + sl * 8192 + f * 1024 + j * 512, OBUF + sl * 8192 + f * 1024 + (j + 1) * 512),
                    ab(OBUF + sl * 8192 + f * 1024 + j * 512, OBUF + sl * 8192 + f * 1024 + (j + 1) * 512),
                    bank(rb)), wait=[tm, ob_tok[sl]], inc=DV)
                setbw(rb, tmu)
                rmul_tok[(f, j)] = tmu
        last_mm_prev = None
        for do in range(NCH):
            pb = (do % 2) * 2
            last = None
            for f in range(NCH):
                for j in range(2):
                    last = p.emit("tensor", lambda e, do=do, f=f, j=j, sl=sl, pb=pb: e.matmul(
                        bank(pb, 2)[:, j * 512:(j + 1) * 512],
                        ab(WPR + f * 1024 + do * 128, WPR + f * 1024 + (do + 1) * 128),
                        ab(OBUF + sl * 8192 + f * 1024 + j * 512, OBUF + sl * 8192 + f * 1024 + (j + 1) * 512),
                        start=(f == 0), stop=(f == NCH - 1)),
                        wait=[rmul_tok[(f, j)], twp] + ([bw(pb)] if f == 0 and j == 0 else []),
                        inc=(PE if f == NCH - 1 and j == 1 else None))
            t_mm = last
            ob_war[sl] = t_mm
            tb = p.emit("scalar", lambda e, do=do, pb=pb: e.activation(
                af(BCA, BCA + 1024), bank(pb, 2), AF.Identity,
                bias=af(WCOL + do, WCOL + do + 1)), wait=[t_mm, bw("bca")], inc=AC)
            setbw(pb, tb)
            if own:
                ta = p.emit("vector", lambda e, do=do, sw=sw: e.tensor_add(
                    af(XRES + do * 2048 + sw * 1024, XRES + do * 2048 + (sw + 1) * 1024),
                    af(BCA, BCA + 1024),
                    af(XRES + do * 2048 + sw * 1024, XRES + do * 2048 + (sw + 1) * 1024)),
                    wait=[tb], inc=DV)
                bank_war["bca"] = ta
            else:
                ta = p.emit("vector", lambda e, do=do: e.tensor_add(
                    af(BCB, BCB + 1024), af(BCA, BCA + 1024),
                    af(FBIG + do * 1024, FBIG + (do + 1) * 1024)),
                    wait=[tb, tpx, bw("bcb")], inc=DV)
                bank_war["bca"] = ta
                tst = p.dma("sync", lambda e, do=do, sw=sw: e.dma_start(
                    out=dS["x1p"][do * 128:(do + 1) * 128, (sw - 2) * 1024:(sw - 1) * 1024],
                    in_=af(BCB, BCB + 1024)), wait=[ta], inc=SQ)
                bank_war["bcb"] = tst
    bank_war["brow"] = rb_war[-1]

    if N_PH < 5:
        finish(p, out, af)
        return

    ffn_phase([("own", 0), ("own", 1), ("pp", 0), ("pp", 1)][:FFN_SWEEPS], dS["x1p"], dS["x2p"])

    if N_PH < 6:
        finish(p, out, af)
        return

    HH6 = BBIG
    QKW6 = BBIG + 4096
    QKST = BBIG + 20480
    XL3B = BBIG + 22528
    SQ6 = BBIG + 26624
    QKB6 = BBIG + 27648
    assert QKB6 + 2048 <= BF_COLS

    t_all = [p.tok(PE), p.tok(DV), p.tok(AC), p.tok(SQ), p.tok(GQ)]
    p.dma("gpsimd", lambda e: e.dma_start(
        out=ab(QKW6, QKW6 + 16384).rearrange("p (d m) -> p d m", d=NCH),
        in_=dI["qkvw_t"].rearrange("(d p) m -> p d m", p=128)[:, :, 0:2048]),
        wait=t_all, inc=GQ)
    t_qkw = p.tok(GQ)
    tqb0 = p.dma("sync", lambda e: e.dma_start(
        out=af(FBIG + 4096, FBIG + 6144, 0, 1), in_=dI["qkvb_qk"][:, :]), wait=t_all, inc=SQ)
    tqb1 = p.emit("vector", lambda e: e.tensor_copy(
        ab(BROW, BROW + 2048, 0, 1), af(FBIG + 4096, FBIG + 6144, 0, 1)),
        wait=[tqb0, bw("brow")], inc=DV)
    for j in range(4):
        tm = p.emit("tensor", lambda e, j=j: e.matmul(
            bank(6), ones_row(), ab(BROW + j * 512, BROW + (j + 1) * 512, 0, 1),
            start=True, stop=True), wait=[tqb1, bw(6)], inc=PE)
        tv = p.emit("vector", lambda e, j=j: e.tensor_copy(
            ab(QKB6 + j * 512, QKB6 + (j + 1) * 512), bank(6)), wait=[tm], inc=DV)
        setbw(6, tv)
    bank_war["brow"] = p.tok(PE)
    t_qkb = p.tok(DV)

    def ln3_group(g, src_fn, wait_src, want_xln3):
        w6 = 512
        sq_mm_tok = [None, None]
        t_sum_mm = None
        t_sumsq_mm = None
        t_cast = []
        for d in range(NCH):
            slot = SQ6 + (d % 2) * w6
            tc = p.emit("vector", lambda e, d=d: e.tensor_copy(
                ab(XL3B + d * w6, XL3B + (d + 1) * w6), src_fn(d)), wait=wait_src, inc=DV)
            t_cast.append(tc)
            wl = list(wait_src) + ([sq_mm_tok[d % 2]] if d >= 2 else [bw("sq6")])
            ts = p.emit("scalar", lambda e, d=d, slot=slot: e.activation(
                ab(slot, slot + w6), src_fn(d), AF.Square), wait=wl, inc=AC)
            t_sum_mm = p.emit("tensor", lambda e, d=d: e.matmul(
                bank(2, 1, 0, 1), ones_col(),
                ab(XL3B + d * w6, XL3B + (d + 1) * w6),
                start=(d == 0), stop=(d == NCH - 1)),
                wait=[tc] + ([bw(2)] if d == 0 else []),
                inc=(PE if d == NCH - 1 else None))
            last = p.emit("tensor", lambda e, d=d, slot=slot: e.matmul(
                bank(4, 1, 0, 1), ones_col(), ab(slot, slot + w6),
                start=(d == 0), stop=(d == NCH - 1)),
                wait=[ts] + ([bw(4)] if d == 0 else []),
                inc=(PE if True else None))
            sq_mm_tok[d % 2] = last
            t_sumsq_mm = last
        w = w6
        t1 = p.emit("vector", lambda e: e.tensor_scalar(
            rows(R_MU, w), bank(2, 1, 0, 1), cst(C_INVD, 1), None, op0=ALU.mult),
            wait=[t_sum_mm], inc=DV)
        t2 = p.emit("vector", lambda e: e.tensor_scalar(
            rows(R_SQ, w), bank(4, 1, 0, 1), cst(C_INVD, 1), None, op0=ALU.mult),
            wait=[t_sumsq_mm], inc=DV)
        setbw(2, t1)
        setbw(4, t2)
        t3 = p.emit("vector", lambda e: e.tensor_mul(rows(R_TMP, w), rows(R_MU, w), rows(R_MU, w)),
                    wait=[t1], inc=DV)
        t4 = p.emit("vector", lambda e: e.tensor_sub(rows(R_A, w), rows(R_SQ, w), rows(R_TMP, w)),
                    wait=[t2, t3], inc=DV)
        t5 = p.emit("scalar", lambda e: e.activation(rows(R_B, w), rows(R_A, w), AF.Ln, bias=cst(C_EPS, 1)),
                    wait=[t4], inc=AC)
        t6 = p.emit("scalar", lambda e: e.activation(rows(R_SQ, w), rows(R_B, w), AF.Exp, scale=-0.5),
                    wait=[t5, t2], inc=AC)
        r7 = p.emit("vector", lambda e: e.tensor_scalar(
            rows(R_TMP, w), rows(R_A, w), cst(C_EPS, 1), None, op0=ALU.add), wait=[t4, t3], inc=DV)
        r8 = p.emit("vector", lambda e: e.reciprocal(rows(R_B, w), rows(R_TMP, w)), wait=[r7, t6], inc=DV)
        r9 = p.emit("vector", lambda e: e.tensor_mul(rows(R_TMP, w), rows(R_A, w), rows(R_B, w)),
                    wait=[r8], inc=DV)
        r10 = p.emit("scalar", lambda e: e.activation(rows(R_B, w), rows(R_TMP, w), AF.Ln, bias=cst(C_EPS, 1)),
                     wait=[r9], inc=AC)
        r11 = p.emit("scalar", lambda e: e.activation(rows(R_TMP, w), rows(R_B, w), AF.Exp, scale=-0.5),
                     wait=[r10], inc=AC)
        r12 = p.emit("vector", lambda e: e.tensor_mul(rows(R_B, w), rows(R_MU, w), rows(R_SQ, w)),
                     wait=[t6, r11], inc=DV)
        r13 = p.emit("vector", lambda e: e.tensor_scalar(
            rows(R_A, w), rows(R_B, w), cst(C_NEG1, 1), None, op0=ALU.mult), wait=[r12], inc=DV)
        t_bc = []
        for (roff, dst, tr, boff) in ((R_SQ, BCA, r12, 0), (R_A, BCB, r13, 512),
                                      (R_TMP, FBIG + 4096, r13, 1024)):
            tcast = p.emit("vector", lambda e, roff=roff, boff=boff: e.tensor_copy(
                ab(BROW + boff, BROW + boff + w, 0, 1), rows(roff, w)),
                wait=[tr, bw("brow")], inc=DV)
            tm = p.emit("tensor", lambda e, boff=boff: e.matmul(
                bank(6), ones_row(), ab(BROW + boff, BROW + boff + w, 0, 1),
                start=True, stop=True), wait=[tcast, bw(6)], inc=PE)
            tv = p.emit("vector", lambda e, dst=dst: e.tensor_copy(
                af(dst, dst + w), bank(6)), wait=[tm], inc=DV)
            setbw(6, tv)
            t_bc.append(tv)
        bank_war["brow"] = t_bc[-1]
        t_hh = []
        t_x3 = []
        for d in range(NCH):
            m1 = p.emit("vector", lambda e, d=d: e.tensor_mul(
                af(SCR, SCR + w), src_fn(d), af(BCA, BCA + w)),
                wait=list(t_bc) + ([t_hh[d - 1]] if d > 0 else []), inc=DV)
            m2 = p.emit("vector", lambda e, d=d: e.tensor_add(
                af(SCR + 512, SCR + 1024), af(SCR, SCR + w), af(BCB, BCB + w)),
                wait=[m1], inc=DV)
            if want_xln3:
                t_x3.append(p.emit("vector", lambda e, d=d: e.tensor_copy(
                    ab(XL3B + d * w, XL3B + (d + 1) * w), af(SCR + 512, SCR + 1024)),
                    wait=[m2], inc=DV))
            m3 = p.emit("vector", lambda e, d=d: e.tensor_mul(
                af(SCR, SCR + w), af(SCR + 512, SCR + 1024), af(FBIG + 4096, FBIG + 4096 + w)),
                wait=[m2], inc=DV)
            t_hh.append(p.emit("scalar", lambda e, d=d: e.activation(
                ab(HH6 + d * w, HH6 + (d + 1) * w), af(SCR, SCR + w), AF.Identity,
                scale=af(WCOL + 24 + d, WCOL + 24 + d + 1),
                bias=af(WCOL + 32 + d, WCOL + 32 + d + 1)), wait=[m3], inc=AC))
        bank_war["sq6"] = t_sumsq_mm
        return t_hh, t_x3

    qkst_war = [None, None]
    hh_war = None
    spill_war = []
    x2src_war = None
    for g in range(8):
        own = g < 4
        w6 = 512
        if own:
            def src_fn(d, g=g):
                return af(XRES + d * 2048 + g * 512, XRES + d * 2048 + (g + 1) * 512)
            t_src = [p.tok(DV)]
        else:
            tl = p.dma("sync", lambda e, g=g: e.dma_start(
                out=af(FBIG, FBIG + 4096).rearrange("p (d t) -> p d t", d=NCH),
                in_=dS["x2p"].rearrange("(d p) t -> p d t", p=128)[:, :, (g - 4) * 512:(g - 3) * 512]),
                wait=[p.tok(DV), p.tok(AC), x2src_war], inc=SQ)

            def src_fn(d):
                return af(FBIG + d * 512, FBIG + (d + 1) * 512)
            t_src = [tl]
        if hh_war is not None:
            t_src = t_src + [hh_war] + spill_war
        t_hh, t_x3 = ln3_group(g, src_fn, t_src, own)
        spill_war = []
        if own:
            ts1 = p.dma("sync", lambda e, g=g: e.dma_start(
                out=dS["xln3o"].rearrange("(d p) t -> p d t", p=128)[:, :, g * 512:(g + 1) * 512],
                in_=ab(XL3B, XL3B + 4096).rearrange("p (d t) -> p d t", d=NCH)),
                wait=[t_x3[-1]], inc=SQ)
            ts2 = p.dma("sync", lambda e, g=g: e.dma_start(
                out=dS["hh_sp"].rearrange("(d p) t -> p d t", p=128)[:, :, g * 512:(g + 1) * 512],
                in_=ab(HH6, HH6 + 4096).rearrange("p (d t) -> p d t", d=NCH)),
                wait=[t_hh[-1]], inc=SQ)
            spill_war = [ts1, ts2]
        last_hh_mm = None
        for tt in range(4):
            last = None
            for d in range(NCH):
                for j in range(4):
                    last = p.emit("tensor", lambda e, d=d, j=j, tt=tt: e.matmul(
                        bank(j), ab(HH6 + d * 512 + tt * 128, HH6 + d * 512 + tt * 128 + 128),
                        ab(QKW6 + d * 2048 + j * 512, QKW6 + d * 2048 + (j + 1) * 512),
                        start=(d == 0), stop=(d == NCH - 1)),
                        wait=[t_hh[d], t_qkw] + ([bw(j)] if d == 0 else []),
                        inc=(PE if d == NCH - 1 and j == 3 else None))
            t_mm = last
            last_hh_mm = t_mm
            ssl = tt % 2
            tv = p.emit("vector", lambda e, ssl=ssl: e.tensor_add(
                ab(QKST + ssl * 1024, QKST + ssl * 1024 + 1024),
                bank(0, 2), ab(QKB6, QKB6 + 1024)),
                wait=[t_mm, t_qkb, qkst_war[ssl]], inc=DV)
            setbw([0, 1], tv)
            td = p.dma("sync", lambda e, g=g, tt=tt, ssl=ssl: e.dma_start(
                out=dS["qb_sp"][g * 512 + tt * 128: g * 512 + (tt + 1) * 128, :],
                in_=ab(QKST + ssl * 1024, QKST + ssl * 1024 + 1024)), wait=[tv], inc=SQ)
            tvk = p.emit("vector", lambda e, ssl=ssl: e.tensor_add(
                ab(QKST + ssl * 1024, QKST + ssl * 1024 + 1024),
                bank(2, 2), ab(QKB6 + 1024, QKB6 + 2048)),
                wait=[t_mm, td], inc=DV)
            setbw([2, 3], tvk)
            tdk = p.dma("sync", lambda e, g=g, tt=tt, ssl=ssl: e.dma_start(
                out=dS["kb_sp"][g * 512 + tt * 128: g * 512 + (tt + 1) * 128, :],
                in_=ab(QKST + ssl * 1024, QKST + ssl * 1024 + 1024)), wait=[tvk], inc=SQ)
            qkst_war[ssl] = tdk
        hh_war = last_hh_mm

    VW6 = BBIG + 4096
    HHG = BBIG + 12288
    VST6 = BBIG + 16384
    t_pa = [p.tok(PE), p.tok(DV), p.tok(AC), p.tok(SQ)]
    t_vw = p.dma("gpsimd", lambda e: e.dma_start(
        out=ab(VW6, VW6 + 8192).rearrange("p (d m) -> p d m", d=NCH),
        in_=dI["qkvw_t"].rearrange("(d p) m -> p d m", p=128)[:, :, 2048:3072]),
        wait=t_pa, inc=GQ)
    vst_war6 = [None, None]
    hhg_tok = None
    hhg_war = None
    for g in range(4):
        tl = p.dma("sync", lambda e, g=g: e.dma_start(
            out=ab(HHG, HHG + 4096).rearrange("p (d t) -> p d t", d=NCH),
            in_=dS["hh_sp"].rearrange("(d p) t -> p d t", p=128)[:, :, g * 512:(g + 1) * 512]),
            wait=t_pa + [hhg_war], inc=SQ)
        last_v = None
        for vf in range(8):
            vb_ = 4 + vf % 2
            last = None
            for d in range(NCH):
                last = p.emit("tensor", lambda e, d=d, vf=vf, vb_=vb_: e.matmul(
                    bank(vb_), ab(VW6 + d * 1024 + vf * 128, VW6 + d * 1024 + (vf + 1) * 128),
                    ab(HHG + d * 512, HHG + (d + 1) * 512),
                    start=(d == 0), stop=(d == NCH - 1)),
                    wait=[tl, t_vw] + ([bw(vb_)] if d == 0 else []),
                    inc=(PE if d == NCH - 1 else None))
            t_mm = last
            last_v = t_mm
            ssl = vf % 2
            tv = p.emit("scalar", lambda e, vf=vf, ssl=ssl, vb_=vb_: e.activation(
                ab(VST6 + ssl * 512, VST6 + (ssl + 1) * 512), bank(vb_), AF.Identity,
                bias=af(WCOL + 8 + vf, WCOL + 8 + vf + 1)),
                wait=[t_mm, vst_war6[ssl]], inc=AC)
            setbw(vb_, tv)
            td = p.dma("sync", lambda e, g=g, vf=vf, ssl=ssl: e.dma_start(
                out=dS["vb_sp"][vf * 128:(vf + 1) * 128, g * 512:(g + 1) * 512],
                in_=ab(VST6 + ssl * 512, VST6 + (ssl + 1) * 512)), wait=[tv], inc=SQ)
            vst_war6[ssl] = td
        hhg_war = last_v

    QG6 = BBIG
    KG6 = BBIG + 8192
    t_pb = [p.tok(PE), p.tok(DV), p.tok(AC), p.tok(SQ)]
    qg_tok = [None, None]
    qg_war = [None, None]
    lg_tok = {}

    def load_qk(g):
        sl = g % 2
        t1 = p.dma("sync", lambda e, g=g, sl=sl: e.dma_start(
            out=ab(QG6 + sl * 4096, QG6 + (sl + 1) * 4096).rearrange("p (c m) -> p c m", c=4),
            in_=dS["qb_sp"].rearrange("(c p) m -> p c m", p=128)[:, g * 4:(g + 1) * 4, :]),
            wait=t_pb + [qg_war[sl]], inc=SQ)
        t2 = p.dma("sync", lambda e, g=g, sl=sl: e.dma_start(
            out=ab(KG6 + sl * 4096, KG6 + (sl + 1) * 4096).rearrange("p (c m) -> p c m", c=4),
            in_=dS["kb_sp"].rearrange("(c p) m -> p c m", p=128)[:, g * 4:(g + 1) * 4, :]),
            inc=SQ)
        qg_tok[sl] = t2

    load_qk(0)
    for g in range(8):
        sl = g % 2
        t_ld = qg_tok[sl]
        if g + 1 < 8:
            load_qk(g + 1)
        last_of_g = None
        for qf in range(8):
            pb6 = (qf % 2) * 2
            last = None
            for tt in range(4):
                for j in range(2):
                    last = p.emit("tensor", lambda e, qf=qf, tt=tt, j=j, sl=sl, pb6=pb6: e.matmul(
                        bank(pb6, 2)[:, j * 512:(j + 1) * 512],
                        ab(QG6 + sl * 4096 + tt * 1024 + qf * 128, QG6 + sl * 4096 + tt * 1024 + (qf + 1) * 128),
                        ab(KG6 + sl * 4096 + tt * 1024 + j * 512, KG6 + sl * 4096 + tt * 1024 + (j + 1) * 512),
                        start=(tt == 0), stop=(tt == 3)),
                        wait=[t_ld] + ([bw(pb6)] if tt == 0 and j == 0 else []),
                        inc=(PE if tt == 3 and j == 1 else None))
            t_mm = last
            if g == 0:
                ta = p.emit("vector", lambda e, qf=qf, pb6=pb6: e.tensor_copy(
                    af(FBIG + qf * 1024, FBIG + (qf + 1) * 1024), bank(pb6, 2)),
                    wait=[t_mm], inc=DV)
            else:
                ta = p.emit("vector", lambda e, qf=qf, pb6=pb6: e.tensor_add(
                    af(FBIG + qf * 1024, FBIG + (qf + 1) * 1024), bank(pb6, 2),
                    af(FBIG + qf * 1024, FBIG + (qf + 1) * 1024)),
                    wait=[t_mm, lg_tok.get(qf)], inc=DV)
            lg_tok[qf] = ta
            setbw([pb6, pb6 + 1], ta)
            last_of_g = t_mm
        qg_war[sl] = last_of_g

    W6 = BBIG + 16384
    WT6 = BBIG
    sm_tok = {}
    for qf in range(8):
        c0 = SCR
        tmx = p.emit("vector", lambda e, qf=qf: e.reduce_max(
            af(SCR, SCR + 1), af(FBIG + qf * 1024, FBIG + (qf + 1) * 1024), axis=AX.X),
            wait=[lg_tok[qf]], inc=DV)
        tng = p.emit("vector", lambda e: e.tensor_scalar(
            af(SCR + 1, SCR + 2), af(SCR, SCR + 1), cst(C_NATTB), None, op0=ALU.mult),
            wait=[tmx], inc=DV)
        tex = p.emit("scalar", lambda e, qf=qf: e.activation(
            ab(W6 + qf * 1024, W6 + (qf + 1) * 1024), af(FBIG + qf * 1024, FBIG + (qf + 1) * 1024),
            AF.Exp, scale=ATTB_SCALE, bias=af(SCR + 1, SCR + 2),
            accum_out=af(SCR + 2, SCR + 3)), wait=[tng, bw("w6")], inc=AC)
        trc = p.emit("vector", lambda e: e.reciprocal(
            af(SCR + 3, SCR + 4), af(SCR + 2, SCR + 3)), wait=[tex], inc=DV)
        tsc = p.emit("vector", lambda e, qf=qf: e.tensor_scalar(
            ab(W6 + qf * 1024, W6 + (qf + 1) * 1024), ab(W6 + qf * 1024, W6 + (qf + 1) * 1024),
            af(SCR + 3, SCR + 4), None, op0=ALU.mult), wait=[trc], inc=DV)
        sm_tok[qf] = tsc
    wt_tok = {}
    for kf in range(8):
        for qf in range(8):
            ttr = p.emit("tensor", lambda e, kf=kf, qf=qf: e.transpose(
                ptb(0, 128), ab(W6 + qf * 1024 + kf * 128, W6 + qf * 1024 + (kf + 1) * 128),
                ab(BIDENT, BIDENT + 128)), wait=[sm_tok[qf], bw("pt")], inc=PE)
            tev = p.emit("vector", lambda e, kf=kf, qf=qf: e.tensor_copy(
                ab(WT6 + kf * 1024 + qf * 128, WT6 + kf * 1024 + (qf + 1) * 128), ptb(0, 128)),
                wait=[ttr], inc=DV)
            bank_war["pt"] = tev
            wt_tok[kf] = tev
    bank_war["w6"] = p.tok(DV)

    VB6 = BBIG + 8192
    ACH = BBIG + 12288
    PJW = BBIG + 16384
    XL36 = BBIG + 24576
    t_pd = [p.tok(PE), p.tok(DV), p.tok(AC), p.tok(SQ)]
    t_pj = p.dma("gpsimd", lambda e: e.dma_start(
        out=ab(PJW, PJW + 8192).rearrange("p (f m) -> p f m", f=NCH),
        in_=dI["projw_t"].rearrange("(f p) m -> p f m", p=128)), wait=t_pd, inc=GQ)
    vb_war = None
    xl_war = None
    for cc in range(4):
        tlv = p.dma("sync", lambda e, cc=cc: e.dma_start(
            out=ab(VB6, VB6 + 4096).rearrange("p (f t) -> p f t", f=NCH),
            in_=dS["vb_sp"].rearrange("(f p) t -> p f t", p=128)[:, :, cc * 512:(cc + 1) * 512]),
            wait=t_pd + [vb_war], inc=SQ)
        tlx = p.dma("sync", lambda e, cc=cc: e.dma_start(
            out=ab(XL36, XL36 + 4096).rearrange("p (d t) -> p d t", d=NCH),
            in_=dS["xln3o"].rearrange("(d p) t -> p d t", p=128)[:, :, cc * 512:(cc + 1) * 512]),
            wait=[xl_war] if xl_war else [], inc=SQ)
        ach_tok = {}
        last_av = None
        for qf in range(8):
            ab6 = qf % 2
            last = None
            for kf in range(8):
                last = p.emit("tensor", lambda e, qf=qf, kf=kf, ab6=ab6: e.matmul(
                    bank(ab6), ab(WT6 + kf * 1024 + qf * 128, WT6 + kf * 1024 + (qf + 1) * 128),
                    ab(VB6 + kf * 512, VB6 + (kf + 1) * 512),
                    start=(kf == 0), stop=(kf == NCH - 1)),
                    wait=[tlv, wt_tok[kf]] + ([bw(ab6)] if kf == 0 else []),
                    inc=(PE if kf == NCH - 1 else None))
            t_mm = last
            last_av = t_mm
            tv = p.emit("vector", lambda e, qf=qf, ab6=ab6: e.tensor_copy(
                ab(ACH + qf * 512, ACH + (qf + 1) * 512), bank(ab6)),
                wait=[t_mm, bw("ach")], inc=DV)
            setbw(ab6, tv)
            ach_tok[qf] = tv
        vb_war = last_av
        last_pj = None
        for do in range(NCH):
            pb6 = 2 + do % 2
            last = None
            for f in range(NCH):
                last = p.emit("tensor", lambda e, do=do, f=f, pb6=pb6: e.matmul(
                    bank(pb6), ab(PJW + f * 1024 + do * 128, PJW + f * 1024 + (do + 1) * 128),
                    ab(ACH + f * 512, ACH + (f + 1) * 512),
                    start=(f == 0), stop=(f == NCH - 1)),
                    wait=[t_pj, ach_tok[f]] + ([bw(pb6)] if f == 0 else []),
                    inc=(PE if f == NCH - 1 else None))
            t_mm = last
            last_pj = t_mm
            tb = p.emit("scalar", lambda e, do=do, pb6=pb6: e.activation(
                af(BCA, BCA + 512), bank(pb6), AF.Identity,
                bias=af(WCOL + 16 + do, WCOL + 16 + do + 1)), wait=[t_mm, bw("bca")], inc=AC)
            setbw(pb6, tb)
            t5a = p.emit("vector", lambda e, do=do: e.tensor_add(
                af(BCB, BCB + 512), af(BCA, BCA + 512), ab(XL36 + do * 512, XL36 + (do + 1) * 512)),
                wait=[tb, tlx, bw("bcb")], inc=DV)
            bank_war["bca"] = t5a
            t5b = p.emit("vector", lambda e, do=do, cc=cc: e.tensor_add(
                af(XRES + do * 2048 + cc * 512, XRES + do * 2048 + (cc + 1) * 512),
                af(BCB, BCB + 512),
                af(XRES + do * 2048 + cc * 512, XRES + do * 2048 + (cc + 1) * 512)),
                wait=[t5a], inc=DV)
            bank_war["bcb"] = t5b
        xl_war = last_pj

    if N_PH < 7:
        finish(p, out, af)
        return

    ffn_phase([("own", 0), ("own", 1)], None, None)

    finish(p, out, af)


def finish(p, out, af):
    td = p.dma("sync", lambda e: e.dma_start(
        out=out.rearrange("(d p) t -> p d t", p=128),
        in_=af(XRES, XRES + 16384).rearrange("p (d t) -> p d t", d=NCH)),
        wait=[p.tok("pe"), p.tok("dv"), p.tok("ac"), p.tok("sq"), p.tok("gq")], inc="sq")
    p.emit("sync", lambda e: None, wait=[td])


def _layout_inputs(inputs, core):
    b, h = core // 2, core % 2
    x = np.asarray(inputs["L0_attn_a_query"])
    xT = np.ascontiguousarray(x[b].T)
    x0 = np.concatenate([xT[:, h * TO:(h + 1) * TO], xT[:, (1 - h) * TO:(2 - h) * TO]], axis=1)
    m = {"x0": np.ascontiguousarray(x0)}
    m["wq"] = np.ascontiguousarray(np.asarray(inputs["L0_attn_a_Wq"]).transpose(1, 0, 2).reshape(D, D))
    m["wk"] = np.ascontiguousarray(np.asarray(inputs["L0_attn_a_Wk"]).transpose(1, 0, 2).reshape(D, D))
    m["wv"] = np.ascontiguousarray(np.asarray(inputs["L0_attn_a_Wv"]).transpose(1, 0, 2).reshape(D, D))
    m["wp"] = np.ascontiguousarray(
        np.asarray(inputs["L0_attn_a_Wp"]).reshape(EH, 16, D).transpose(1, 0, 2).reshape(D, D))
    m["bp_cols"] = np.ascontiguousarray(np.asarray(inputs["L0_attn_a_bp"]).reshape(8, 128).T)
    m["qkvw_t"] = np.ascontiguousarray(np.asarray(inputs["L0_attn_b_qkv_w"]).T)
    m["qkvb_qk"] = np.ascontiguousarray(np.asarray(inputs["L0_attn_b_qkv_b"])[None, 0:2 * D])
    m["qkvb_v_cols"] = np.ascontiguousarray(np.asarray(inputs["L0_attn_b_qkv_b"])[2 * D:].reshape(8, 128).T)
    m["projw_t"] = np.ascontiguousarray(np.asarray(inputs["L0_attn_b_proj_w"]).T)
    m["projb_cols"] = np.ascontiguousarray(np.asarray(inputs["L0_attn_b_proj_b"]).reshape(8, 128).T)
    m["nw_cols"] = np.ascontiguousarray(np.asarray(inputs["L0_attn_b_norm_w"]).reshape(8, 128).T)
    m["nb_cols"] = np.ascontiguousarray(np.asarray(inputs["L0_attn_b_norm_b"]).reshape(8, 128).T)
    m["wg"] = np.ascontiguousarray(np.asarray(inputs["L0_ffn_Wg"]))
    m["w1"] = np.ascontiguousarray(np.asarray(inputs["L0_ffn_W1"]))
    m["b1_cols"] = np.ascontiguousarray(
        np.asarray(inputs["L0_ffn_b1"]).reshape(NE, 16, 128).transpose(2, 0, 1).reshape(128, 128))
    m["w2"] = np.ascontiguousarray(np.asarray(inputs["L0_ffn_W2"]))
    m["b2r"] = np.ascontiguousarray(np.asarray(inputs["L0_ffn_b2"]))
    m["ident"] = np.eye(128, dtype=ml_dtypes.bfloat16)
    oh = np.zeros((NE, NE * 128), dtype=ml_dtypes.bfloat16)
    for e in range(NE):
        oh[e, e * 128:(e + 1) * 128] = 1
    m["onehot"] = oh
    s2 = np.zeros((16, 1024), dtype=ml_dtypes.bfloat16)
    for hp in range(8):
        s2[2 * hp, hp * 128:hp * 128 + 64] = 1
        s2[2 * hp + 1, hp * 128 + 64:hp * 128 + 128] = 1
    m["sel2"] = s2
    return m


def kernel(**inputs):
    nc = build_nc()
    in_maps = [_layout_inputs(inputs, c) for c in range(8)]
    res = run_bass_kernel_spmd(nc, in_maps, core_ids=list(range(8)))
    y = np.zeros((4, TF, D), dtype=np.float32)
    for c in range(8):
        b, h = c // 2, c % 2
        y[b, h * TO:(h + 1) * TO, :] = np.asarray(res.results[c]["out"]).T
    return y


# version 12
# speedup vs baseline: 1.0695x; 1.0695x over previous
import contextlib
import numpy as np
import ml_dtypes
import concourse.bass as bass
import concourse.mybir as mybir
from concourse.bass_utils import run_bass_kernel_spmd

F32 = mybir.dt.float32
BF16 = mybir.dt.bfloat16
AF = mybir.ActivationFunctionType
ALU = mybir.AluOpType
AX = mybir.AxisListType

N_PH = 99
DEBUG = False
FFN_SWEEPS = 4
FFN_NE = 8
FFN_GATE_ONLY = False

D = 1024
NCH = 8
TF = 4096
TO = 2048
NHP = 8
EH = 64
HID = 2048
NE = 8
EPS = 1e-5
ATT_SCALE = 0.125
ATTB_SCALE = 1.0 / 64.0

ENGINES = ("tensor", "vector", "scalar", "gpsimd", "sync")


class Prog:
    def __init__(self):
        self.ops = {e: [] for e in ENGINES}
        self.sem_vals = {}
        self.last_wait = {e: {} for e in ENGINES}

    def _sem(self, name):
        if name not in self.sem_vals:
            self.sem_vals[name] = 0

    def emit(self, eng, fn, wait=(), inc=None, inc_n=1):
        waits = []
        lw = self.last_wait[eng]
        for t in wait:
            if t is None:
                continue
            s, v = t
            if v <= 0:
                continue
            if s in lw and lw[s] >= v:
                continue
            merged = False
            for i, (s2, v2) in enumerate(waits):
                if s2 == s:
                    waits[i] = (s, max(v, v2))
                    merged = True
            if not merged:
                waits.append((s, v))
        for s, v in waits:
            lw[s] = max(lw.get(s, 0), v)
        tok = None
        incs = []
        if inc is not None:
            self._sem(inc)
            self.sem_vals[inc] += inc_n
            tok = (inc, self.sem_vals[inc])
            incs.append((inc, inc_n))
        self.ops[eng].append((fn, waits, incs))
        return tok

    def dma(self, eng, fn, wait=(), inc=None):
        return self.emit(eng, fn, wait=wait, inc=inc, inc_n=16)

    def tok(self, sem):
        self._sem(sem)
        return (sem, self.sem_vals[sem])

    def build(self, nc, stack):
        sems = {name: stack.enter_context(nc.semaphore(name))
                for name in self.sem_vals}
        block = stack.enter_context(nc.Block())

        def mk(eng_name):
            def body(engine):
                for fn, waits, incs in self.ops[eng_name]:
                    for s, v in waits:
                        engine.wait_ge(sems[s], v)
                    inst = fn(engine)
                    for s, n in incs:
                        inst = inst.then_inc(sems[s], n)
            return body

        for e in ENGINES:
            if self.ops[e]:
                getattr(block, e)(mk(e))


XRES = 0
CONST = 16384
WCOL = 16416
B2C8 = 16608
ROWS = 17632
BCA = 22752
BCB = 23776
SCR = 24800
FBIG = 26464
F32_COLS = 34688

R_MU = 0
R_SQ = 1024
R_TMP = 2048
R_A = 3072
R_B = 4096
NRM = SCR + 1536

BIDENT = 0
BONEH = 128
BSROW = 1152
BONES = 3200
BMISC = 3328
BROW = 3584
B2B = 5632
BBIG = 6656
BF_COLS = 36864

CONST_VALS = {0: EPS, 1: 1.0 / D, 2: -1.0, 3: -ATT_SCALE, 4: -ATTB_SCALE,
              5: 1.0, 6: 0.5, 7: 1.0 / NE}
C_EPS, C_INVD, C_NEG1, C_NATT, C_NATTB, C_ONE, C_HALF, C_INV8 = range(8)

MCOL = WCOL + 176


def build_nc():
    nc = bass.Bass("TRN2", target_bir_lowering=False)
    p = Prog()

    dI = {}
    for name, shape, dt in [
        ("x0", [D, TF], F32),
        ("wq", [D, D], F32), ("wk", [D, D], F32), ("wv", [D, D], F32),
        ("wp", [D, D], F32), ("bp_cols", [128, 8], F32),
        ("qkvw_t", [D, 3 * D], F32), ("qkvb_qk", [1, 2 * D], F32),
        ("qkvb_v_cols", [128, 8], F32),
        ("projw_t", [D, D], F32), ("projb_cols", [128, 8], F32),
        ("nw_cols", [128, 8], F32), ("nb_cols", [128, 8], F32),
        ("wg", [D, NE], F32),
        ("w1", [NE, D, HID], F32), ("b1_cols", [128, 128], F32),
        ("w2", [NE, HID, D], F32), ("b2r", [NE, D], F32),
        ("ident", [128, 128], BF16), ("onehot", [NE, NE * 128], BF16),
        ("sel2", [16, 1024], BF16),
    ]:
        dI[name] = nc.dram_tensor(name, shape, dt, kind="ExternalInput").ap()
    out = nc.dram_tensor("out", [D, TO], F32, kind="ExternalOutput").ap()

    skind = "ExternalOutput" if DEBUG else "Internal"
    dS = {}
    for name, shape, dt in [
        ("q_sp", [D, TF], BF16), ("k_sp", [D, TF], BF16),
        ("v_sp", [NHP, TF, 130], BF16), ("o_sp", [D, TF], BF16),
        ("x1p", [D, TO], F32), ("x2p", [D, TO], F32),
        ("xln3o", [D, TO], BF16),
        ("qb_sp", [TF, D], BF16), ("kb_sp", [TF, D], BF16),
        ("w1b", [NE, D, HID], BF16), ("w2b", [NE, HID, D], BF16),
        ("vb_sp", [D, TO], BF16), ("hh_sp", [D, TO], BF16),
        ("dn_sp", [16, TF], F32),
        ("dbg", [D, TF], F32), ("dbg2", [128, 64], F32),
    ]:
        dS[name] = nc.dram_tensor(name, shape, dt, kind=skind).ap()

    with contextlib.ExitStack() as stack:
        AFt = stack.enter_context(nc.sbuf_tensor("AF", [128, F32_COLS], F32))
        ABt = stack.enter_context(nc.sbuf_tensor("AB", [128, BF_COLS], BF16))
        PS = stack.enter_context(nc.psum_tensor("PS", [128, 3584], F32))
        PT = stack.enter_context(nc.psum_tensor("PTb", [128, 1024], BF16))

        def af(c0, c1, p0=0, p1=128):
            return AFt[p0:p1, c0:c1]

        def ab(c0, c1, p0=0, p1=128):
            return ABt[p0:p1, c0:c1]

        def bank(i, n=1, p0=0, p1=128):
            return PS[p0:p1, i * 512:(i + n) * 512]

        def ptb(c0, c1, p0=0, p1=128):
            return PT[p0:p1, c0:c1]

        build_phases(nc, p, dI, dS, out, af, ab, bank, ptb)

        with contextlib.ExitStack() as st2:
            p.build(nc, st2)
    return nc


def build_phases(nc, p, dI, dS, out, af, ab, bank, ptb):
    PE, DV, AC, GP, SQ, GQ = "pe", "dv", "ac", "gp", "sq", "gq"
    bank_war = {}

    def bw(i):
        return bank_war.get(i)

    def setbw(i, tok):
        for k in (i if isinstance(i, (list, tuple)) else [i]):
            bank_war[k] = tok

    def rows(off, n, w=1):
        return af(ROWS + off, ROWS + off + n, 0, w)

    def cst(i, np_=128):
        return af(CONST + i, CONST + i + 1, 0, np_)

    def ones_col():
        return ab(BONES, BONES + 1)

    def ones_row(m=128):
        return ab(BONES, BONES + m, 0, 1)

    for idx, val in CONST_VALS.items():
        p.emit("vector", lambda e, i=idx, v=val: e.memset(af(CONST + i, CONST + i + 1), v), inc=DV)
    p.emit("vector", lambda e: e.memset(ab(BONES, BONES + 128), 1.0), inc=DV)
    tz0 = p.emit("vector", lambda e: e.memset(ab(BMISC, BMISC + 2), 0.0), inc=DV)
    tz1 = p.emit("vector", lambda e: e.memset(ab(BMISC, BMISC + 1, 0, 64), 1.0), wait=[tz0], inc=DV)
    p.emit("vector", lambda e: e.memset(ab(BMISC + 1, BMISC + 2, 64, 128), 1.0), wait=[tz1], inc=DV)
    p.emit("vector", lambda e: e.memset(af(NRM, NRM + 64), 0.0), inc=DV)
    p.dma("sync", lambda e: e.dma_start(out=ab(BIDENT, BIDENT + 128), in_=dI["ident"][:, :]), inc=SQ)
    p.dma("sync", lambda e: e.dma_start(out=ab(BONEH, BONEH + 1024, 0, NE), in_=dI["onehot"][:, :]), inc=SQ)
    p.dma("sync", lambda e: e.dma_start(out=ab(BSROW, BSROW + 1024, 0, 16), in_=dI["sel2"][:, :]), inc=SQ)
    for i, nm in enumerate(["bp_cols", "qkvb_v_cols", "projb_cols", "nw_cols", "nb_cols"]):
        p.dma("sync", lambda e, nm=nm, i=i: e.dma_start(
            out=af(WCOL + 8 * i, WCOL + 8 * i + 8), in_=dI[nm][:, :]), inc=SQ)
    p.dma("sync", lambda e: e.dma_start(out=af(WCOL + 48, WCOL + 176), in_=dI["b1_cols"][:, :]), inc=SQ)
    p.dma("gpsimd", lambda e: e.dma_start(out=ab(B2B, B2B + 1024, 0, NE), in_=dI["b2r"][:, :]), inc=GQ)
    p.dma("sync", lambda e: e.dma_start(
        out=af(XRES, XRES + 16384).rearrange("p (d t) -> p d t", d=NCH),
        in_=dI["x0"].rearrange("(d p) t -> p d t", p=128)[:, :, 0:TO]), inc=SQ)
    t_init = p.tok(SQ)
    t_initv = p.tok(DV)

    def ln_sweep(w, src_fn, xln, sqb, wait_src, war_sq=(), war_brow=()):
        nh = w // 512
        sq_mm_tok = [None, None]
        t_cast = []
        t_sum_mm = None
        t_sumsq_mm = None
        for d in range(NCH):
            slot = sqb + (d % 2) * w
            tc = p.emit("vector", lambda e, d=d: e.tensor_copy(
                ab(xln + d * w, xln + (d + 1) * w), src_fn(d)), wait=wait_src, inc=DV)
            t_cast.append(tc)
            wl = list(wait_src) + list(war_sq if d < 2 else ()) + \
                ([sq_mm_tok[d % 2]] if d >= 2 else [])
            ts = p.emit("scalar", lambda e, d=d, slot=slot: e.activation(
                ab(slot, slot + w), src_fn(d), AF.Square), wait=wl, inc=AC)
            for j in range(nh):
                t_sum_mm = p.emit("tensor", lambda e, d=d, j=j: e.matmul(
                    bank(2, 2, 0, 1)[:, j * 512:(j + 1) * 512], ones_col(),
                    ab(xln + d * w + j * 512, xln + d * w + (j + 1) * 512),
                    start=(d == 0), stop=(d == NCH - 1)),
                    wait=[tc, t_initv] + ([bw(2)] if d == 0 and j == 0 else []),
                    inc=(PE if d == NCH - 1 and j == nh - 1 else None))
            last = None
            for j in range(nh):
                last = p.emit("tensor", lambda e, d=d, j=j, slot=slot: e.matmul(
                    bank(4, 2, 0, 1)[:, j * 512:(j + 1) * 512], ones_col(),
                    ab(slot + j * 512, slot + (j + 1) * 512),
                    start=(d == 0), stop=(d == NCH - 1)),
                    wait=[ts] + ([bw(4)] if d == 0 and j == 0 else []),
                    inc=(PE if j == nh - 1 else None))
            sq_mm_tok[d % 2] = last
            t_sumsq_mm = last
        t1 = p.emit("vector", lambda e: e.tensor_scalar(
            rows(R_MU, w), bank(2, 2, 0, 1)[:, 0:w], cst(C_INVD, 1), None, op0=ALU.mult),
            wait=[t_sum_mm], inc=DV)
        t2 = p.emit("vector", lambda e: e.tensor_scalar(
            rows(R_SQ, w), bank(4, 2, 0, 1)[:, 0:w], cst(C_INVD, 1), None, op0=ALU.mult),
            wait=[t_sumsq_mm], inc=DV)
        setbw([2, 3], t1)
        setbw([4, 5], t2)
        t3 = p.emit("vector", lambda e: e.tensor_mul(rows(R_TMP, w), rows(R_MU, w), rows(R_MU, w)),
                    wait=[t1], inc=DV)
        t4 = p.emit("vector", lambda e: e.tensor_sub(rows(R_A, w), rows(R_SQ, w), rows(R_TMP, w)),
                    wait=[t2, t3], inc=DV)
        t5 = p.emit("scalar", lambda e: e.activation(rows(R_B, w), rows(R_A, w), AF.Ln, bias=cst(C_EPS, 1)),
                    wait=[t4], inc=AC)
        t6 = p.emit("scalar", lambda e: e.activation(rows(R_A, w), rows(R_B, w), AF.Exp, scale=-0.5),
                    wait=[t5], inc=AC)
        t6b = p.emit("vector", lambda e: e.tensor_scalar(
            rows(R_TMP, w), rows(R_A, w), cst(C_NEG1, 1), None, op0=ALU.mult), wait=[t6], inc=DV)
        t7 = p.emit("vector", lambda e: e.tensor_mul(rows(R_B, w), rows(R_MU, w), rows(R_TMP, w)),
                    wait=[t6b], inc=DV)
        t_bc = []
        for bi, (roff, dst, tr) in enumerate(((R_A, BCA, t6), (R_B, BCB, t7))):
            tcast = p.emit("vector", lambda e, roff=roff, bi=bi: e.tensor_copy(
                ab(BROW + bi * 1024, BROW + bi * 1024 + w, 0, 1), rows(roff, w)),
                wait=[tr] + list(war_brow), inc=DV)
            for j in range(nh):
                tm = p.emit("tensor", lambda e, bi=bi, j=j: e.matmul(
                    bank(6), ones_row(),
                    ab(BROW + bi * 1024 + j * 512, BROW + bi * 1024 + (j + 1) * 512, 0, 1),
                    start=True, stop=True), wait=[tcast, bw(6)], inc=PE)
                tv = p.emit("vector", lambda e, dst=dst, j=j: e.tensor_copy(
                    af(dst + j * 512, dst + (j + 1) * 512), bank(6)), wait=[tm], inc=DV)
                setbw(6, tv)
                t_bc.append(tv)
        t_apply = []
        for d in range(NCH):
            tm = p.emit("vector", lambda e, d=d: e.tensor_mul(
                af(SCR, SCR + w), src_fn(d), af(BCA, BCA + w)), wait=t_bc, inc=DV)
            t_apply.append(p.emit("vector", lambda e, d=d: e.tensor_add(
                ab(xln + d * w, xln + (d + 1) * w), af(SCR, SCR + w), af(BCB, BCB + w)),
                wait=[tm], inc=DV))
        return t_apply

    if N_PH < 1:
        finish(p, out, af)
        return

    XLN = BBIG
    SQB = XLN + 8192
    STG = SQB
    NSQ = SQB + 2048
    WQS = NSQ + 1024
    WKS = WQS + 2048
    WVR = WKS + 2048
    VST = WVR + 8192
    assert VST + 1040 <= BF_COLS, VST + 1040

    p.dma("gpsimd", lambda e: e.dma_start(
        out=ab(WVR, WVR + 8192).rearrange("p (d m) -> p d m", d=NCH),
        in_=dI["wv"].rearrange("(d p) m -> p d m", p=128)), inc=GQ)
    t_wv = p.tok(GQ)

    PCW = VST + 1040 + 16
    assert PCW + 4096 <= BF_COLS
    pcw_chunks = []
    for ei in range(NE):
        for srcw, dstw, a in ((dI["w1"], dS["w1b"], NCH), (dI["w2"], dS["w2b"], 16)):
            for pc in range(8):
                pcw_chunks.append((srcw, dstw, ei, a, pc))
    pcw_state = {"i": 0, "war": [None, None], "last": None}

    def emit_precast(n):
        for _ in range(n):
            i = pcw_state["i"]
            if i >= len(pcw_chunks):
                return
            pcw_state["i"] = i + 1
            srcw, dstw, ei, a, pc = pcw_chunks[i]
            sl2 = i % 2
            c = a // 8
            tl = p.dma("gpsimd", lambda e, srcw=srcw, ei=ei, pc=pc, sl2=sl2, c=c: e.dma_start(
                out=ab(PCW + sl2 * 2048, PCW + (sl2 + 1) * 2048).rearrange("p (c i) -> p c i", c=c),
                in_=srcw[ei].rearrange("(c p) i -> p c i", p=128)[:, pc * c:(pc + 1) * c, :]),
                wait=[pcw_state["war"][sl2]], inc=GQ)
            ts = p.dma("sync", lambda e, dstw=dstw, ei=ei, pc=pc, sl2=sl2, c=c: e.dma_start(
                out=dstw[ei].rearrange("(c p) i -> p c i", p=128)[:, pc * c:(pc + 1) * c, :],
                in_=ab(PCW + sl2 * 2048, PCW + (sl2 + 1) * 2048).rearrange("p (c i) -> p c i", c=c)),
                wait=[tl], inc=SQ)
            pcw_state["war"][sl2] = ts
            pcw_state["last"] = ts
    t_vones = p.emit("vector", lambda e: e.memset(
        ab(VST, VST + 1040).rearrange("p (h f) -> p h f", h=16)[:, :, 64:65], 1.0), inc=DV)

    wq_tok = [None, None]; wk_tok = [None, None]
    wq_war = [None, None]; wk_war = [None, None]
    stg_war = [None, None]
    dbg_war = []
    vst_war = t_vones
    xln_war = None
    sq_region_war = []

    w = 1024
    for s in range(4):
        own = s < 2
        if own:
            def src_fn(d, s=s):
                return af(XRES + d * 2048 + (s % 2) * w, XRES + d * 2048 + (s % 2) * w + w)
            t_src = [t_init]
        else:
            tl = p.dma("sync", lambda e, s=s: e.dma_start(
                out=af(FBIG, FBIG + 8192).rearrange("p (d t) -> p d t", d=NCH),
                in_=dI["x0"].rearrange("(d p) t -> p d t", p=128)[:, :, TO + (s - 2) * w: TO + (s - 1) * w]),
                wait=[p.tok(DV), p.tok(AC)], inc=SQ)

            def src_fn(d):
                return af(FBIG + d * w, FBIG + (d + 1) * w)
            t_src = [tl]
        if xln_war is not None:
            t_src = t_src + [xln_war] + list(dbg_war)

        t_apply = ln_sweep(w, src_fn, XLN, SQB, t_src, war_sq=sq_region_war)
        sq_region_war = []
        if DEBUG:
            tdbg = p.dma("gpsimd", lambda e, s=s: e.dma_start(
                out=dS["dbg"].rearrange("(d p) t -> p d t", p=128)[:, :, s * w:(s + 1) * w],
                in_=ab(XLN, XLN + 8192).rearrange("p (d t) -> p d t", d=NCH)),
                wait=[t_apply[-1]], inc=GQ)
            sq_region_war.append(tdbg)
            dbg_war = [tdbg]

        last_xln_mm = None
        for wi, (wdram, slots, stoks, swars, spill) in enumerate((
                (dI["wq"], WQS, wq_tok, wq_war, dS["q_sp"]),
                (dI["wk"], WKS, wk_tok, wk_war, dS["k_sp"]))):
            for hp in range(NHP):
                sl = hp % 2
                if True:
                    tw = p.dma("gpsimd", lambda e, hp=hp, sl=sl, wdram=wdram, slots=slots: e.dma_start(
                        out=ab(slots + sl * 1024, slots + (sl + 1) * 1024).rearrange(
                            "p (d m) -> p d m", d=NCH),
                        in_=wdram.rearrange("(d p) m -> p d m", p=128)[:, :, hp * 128:(hp + 1) * 128]),
                        wait=[swars[sl]], inc=GQ)
                    stoks[sl] = tw
                last = None
                for d in range(NCH):
                    for j in range(2):
                        last = p.emit("tensor", lambda e, d=d, j=j, sl=sl, slots=slots: e.matmul(
                            bank(0, 2)[:, j * 512:(j + 1) * 512],
                            ab(slots + sl * 1024 + d * 128, slots + sl * 1024 + (d + 1) * 128),
                            ab(XLN + d * w + j * 512, XLN + d * w + (j + 1) * 512),
                            start=(d == 0), stop=(d == NCH - 1)),
                            wait=[t_apply[d], stoks[sl]] + ([bw(0)] if d == 0 and j == 0 else []),
                            inc=(PE if d == NCH - 1 and j == 1 else None))
                t_mm = last
                swars[sl] = t_mm
                last_xln_mm = t_mm
                ssl = hp % 2
                tv = p.emit("vector", lambda e, ssl=ssl: e.tensor_copy(
                    ab(STG + ssl * 1024, STG + (ssl + 1) * 1024), bank(0, 2)),
                    wait=[t_mm, stg_war[ssl]], inc=DV)
                setbw([0, 1], tv)
                td = p.dma("sync", lambda e, hp=hp, s=s, ssl=ssl, spill=spill: e.dma_start(
                    out=spill[hp * 128:(hp + 1) * 128, s * w:(s + 1) * w],
                    in_=ab(STG + ssl * 1024, STG + (ssl + 1) * 1024)), wait=[tv], inc=SQ)
                stg_war[ssl] = td
                sq_region_war.append(td)
                tqs = p.emit("scalar", lambda e, ssl=ssl: e.activation(
                    ab(NSQ, NSQ + 1024), ab(STG + ssl * 1024, STG + (ssl + 1) * 1024), AF.Square),
                    wait=[tv, (bank_war.get("nsq"))], inc=AC)
                bank_war["nsq"] = None
                for head in range(2):
                    lastn = None
                    for j in range(2):
                        lastn = p.emit("tensor", lambda e, head=head, j=j: e.matmul(
                            bank(2, 2, 32 * head, 32 * head + 1)[:, j * 512:(j + 1) * 512],
                            ab(BMISC + head, BMISC + head + 1),
                            ab(NSQ + j * 512, NSQ + (j + 1) * 512),
                            start=True, stop=True),
                            wait=[tqs] + ([bw(2)] if j == 0 else []),
                            inc=(PE if j == 1 else None))
                    nidx = NRM + wi * 16 + hp * 2 + head
                    tr1 = p.emit("vector", lambda e, head=head: e.reduce_max(
                        af(SCR + 1024, SCR + 1025, 0, 1),
                        bank(2, 2, 32 * head, 32 * head + 1)[:, 0:1024], axis=AX.X),
                        wait=[lastn], inc=DV)
                    tr2 = p.emit("vector", lambda e, nidx=nidx: e.tensor_max(
                        af(nidx, nidx + 1, 0, 1), af(nidx, nidx + 1, 0, 1),
                        af(SCR + 1024, SCR + 1025, 0, 1)), wait=[tr1, t_initv], inc=DV)
                    setbw([2, 3], tr2)
                bank_war["nsq"] = p.tok(PE)
                emit_precast(2)

        for tt in range(8):
            last = None
            for d in range(NCH):
                for j in range(2):
                    last = p.emit("tensor", lambda e, tt=tt, d=d, j=j: e.matmul(
                        bank(0, 2)[:, j * 512:(j + 1) * 512],
                        ab(XLN + d * w + tt * 128, XLN + d * w + tt * 128 + 128),
                        ab(WVR + d * 1024 + j * 512, WVR + d * 1024 + (j + 1) * 512),
                        start=(d == 0), stop=(d == NCH - 1)),
                        wait=[t_apply[d], t_wv] + ([bw(0)] if d == 0 and j == 0 else []),
                        inc=(PE if d == NCH - 1 and j == 1 else None))
            t_mm = last
            last_xln_mm = t_mm
            tv = p.emit("vector", lambda e: e.tensor_copy(
                ab(VST, VST + 1040).rearrange("p (a b c) -> p a b c", a=8, c=65)[:, :, :, 0:64],
                bank(0, 2).rearrange("p (a b c) -> p a b c", a=8, c=64)),
                wait=[t_mm, vst_war], inc=DV)
            setbw([0, 1], tv)
            td = p.dma("sync", lambda e, s=s, tt=tt: e.dma_start(
                out=dS["v_sp"][:, s * w + tt * 128: s * w + tt * 128 + 128, :].rearrange("a r f -> r a f"),
                in_=ab(VST, VST + 1040).rearrange("p (a f) -> p a f", a=8)),
                wait=[tv], inc=SQ)
            vst_war = td
            sq_region_war.append(td)
        xln_war = last_xln_mm

    for h in range(16):
        hp, head = h // 2, h % 2
        qn = af(NRM + hp * 2 + head, NRM + hp * 2 + head + 1, 0, 1)
        kn = af(NRM + 16 + hp * 2 + head, NRM + 16 + hp * 2 + head + 1, 0, 1)
        sc = af(SCR + 1024 + 1 + h, SCR + 1024 + 2 + h, 0, 1)
        t1 = p.emit("vector", lambda e, qn=qn, kn=kn, sc=sc: e.tensor_mul(sc, qn, kn), inc=DV)
        t2 = p.emit("scalar", lambda e, sc=sc: e.activation(sc, sc, AF.Ln, bias=cst(C_EPS, 1)),
                    wait=[t1], inc=AC)
        t3 = p.emit("scalar", lambda e, sc=sc: e.activation(sc, sc, AF.Exp, scale=0.5),
                    wait=[t2], inc=AC)
        t3b = p.emit("vector", lambda e, sc=sc, h=h: e.tensor_copy(
            ab(BROW + h, BROW + h + 1, 0, 1), sc), wait=[t3], inc=DV)
        tm = p.emit("tensor", lambda e, h=h: e.matmul(
            bank(6)[:, 0:1], ones_row(), ab(BROW + h, BROW + h + 1, 0, 1),
            start=True, stop=True), wait=[t3b, bw(6)], inc=PE)
        tv = p.emit("vector", lambda e, h=h: e.tensor_scalar(
            af(MCOL + h, MCOL + h + 1), bank(6)[:, 0:1], cst(C_NATT), None, op0=ALU.mult),
            wait=[tm], inc=DV)
        setbw(6, tv)
    if DEBUG:
        p.dma("sync", lambda e: e.dma_start(
            out=dS["dbg2"][:, 0:16], in_=af(MCOL, MCOL + 16)), wait=[p.tok(DV)], inc=SQ)
    t_mcol = p.tok(DV)

    assert pcw_state["i"] == len(pcw_chunks), pcw_state["i"]
    t_precast = pcw_state["last"]

    def ffn_phase(sweeps, srcp, dstp):
        XLN5 = BBIG
        SQB5 = BBIG + 8192
        HPR = BBIG + 10240
        W1S1 = BBIG + 18432
        W2B = BBIG + 20480
        SBC = BBIG + 28672
        HTM = BBIG + 29696
        WGS = BMISC + 8
        t_wg = p.dma("gpsimd", lambda e: e.dma_start(
            out=ab(WGS, WGS + 64).rearrange("p (d m) -> p d m", d=NCH),
            in_=dI["wg"].rearrange("(d p) m -> p d m", p=128)),
            wait=[p.tok(PE)], inc=GQ)
        w1_tok = [None] * 4; w1_war = [None] * 4
        w2_tok = [None, None, None, None]; w2_war = [None, None, None, None]
        sbc_war = [None, None]
        htm_war = [None]
        hpr_tok = [None] * 16
        w2done_prev = None

        for s5i, (kind, idx) in enumerate(sweeps):
            own = kind == "own"
            w = 1024
            if own:
                def src_fn(d, idx=idx):
                    return af(XRES + d * 2048 + idx * w, XRES + d * 2048 + idx * w + w)
                t_src = [p.tok(PE), p.tok(DV)]
            else:
                tl = p.dma("sync", lambda e, idx=idx, srcp=srcp: e.dma_start(
                    out=af(FBIG, FBIG + 8192).rearrange("p (d t) -> p d t", d=NCH),
                    in_=srcp.rearrange("(d p) t -> p d t", p=128)[:, :, idx * w:(idx + 1) * w]),
                    wait=[p.tok(DV), p.tok(AC)], inc=SQ)

                def src_fn(d):
                    return af(FBIG + d * w, FBIG + (d + 1) * w)
                t_src = [tl]
            t_apply = ln_sweep(w, src_fn, XLN5, SQB5, t_src,
                               war_sq=[w1_war[0], w1_war[1], p.tok(DV)],
                               war_brow=[bank_war.get("brow")])
            t_xln_ready = t_apply

            for tt in range(8):
                lastg = None
                for d in range(NCH):
                    lastg = p.emit("tensor", lambda e, tt=tt, d=d: e.matmul(
                        bank(0)[:, 0:8],
                        ab(XLN5 + d * w + tt * 128, XLN5 + d * w + tt * 128 + 128),
                        ab(WGS + d * 8, WGS + (d + 1) * 8, 0, 128),
                        start=(d == 0), stop=(d == NCH - 1)),
                        wait=[t_apply[d], t_wg] + ([bw(0)] if d == 0 else []),
                        inc=(PE if d == NCH - 1 else None))
                g = SCR
                tmx = p.emit("vector", lambda e: e.reduce_max(
                    af(g + 16, g + 17), bank(0)[:, 0:8], axis=AX.X), wait=[lastg], inc=DV)
                tng = p.emit("vector", lambda e: e.tensor_scalar(
                    af(g + 17, g + 18), af(g + 16, g + 17), cst(C_NEG1), None, op0=ALU.mult),
                    wait=[tmx], inc=DV)
                tex = p.emit("scalar", lambda e: e.activation(
                    af(g, g + 8), bank(0)[:, 0:8], AF.Exp, bias=af(g + 17, g + 18)),
                    wait=[tng], inc=AC)
                setbw(0, tex)
                tm1 = p.emit("vector", lambda e: e.reduce_max(
                    af(g + 16, g + 17), af(g, g + 8), axis=AX.X), wait=[tex], inc=DV)
                teq = p.emit("vector", lambda e: e.tensor_scalar(
                    af(g + 8, g + 16), af(g, g + 8), af(g + 16, g + 17), None, op0=ALU.is_ge),
                    wait=[tm1], inc=DV)
                tms = p.emit("vector", lambda e: e.tensor_sub(
                    af(g + 8, g + 16), af(g, g + 8), af(g + 8, g + 16)), wait=[teq], inc=DV)
                tm2 = p.emit("vector", lambda e: e.reduce_max(
                    af(g + 18, g + 19), af(g + 8, g + 16), axis=AX.X), wait=[tms], inc=DV)
                tmk = p.emit("vector", lambda e: e.tensor_scalar(
                    af(g + 8, g + 16), af(g, g + 8), af(g + 18, g + 19), None, op0=ALU.is_ge),
                    wait=[tm2], inc=DV)
                tdn = p.emit("vector", lambda e: e.tensor_scalar(
                    af(g + 18, g + 19), af(g + 18, g + 19), cst(C_ONE), None, op0=ALU.add),
                    wait=[tmk], inc=DV)
                trc = p.emit("vector", lambda e: e.reciprocal(
                    af(g + 19, g + 20), af(g + 18, g + 19)), wait=[tdn], inc=DV)
                tsc = p.emit("vector", lambda e: e.tensor_mul(
                    af(g, g + 8), af(g, g + 8), af(g + 8, g + 16)), wait=[tmk], inc=DV)
                tsc2 = p.emit("vector", lambda e: e.tensor_scalar(
                    af(g, g + 8), af(g, g + 8), af(g + 19, g + 20), None, op0=ALU.mult),
                    wait=[trc, tsc], inc=DV)
                tbf = p.emit("vector", lambda e: e.tensor_copy(
                    ab(BROW, BROW + 8), af(g, g + 8)), wait=[tsc2], inc=DV)
                ttr = p.emit("tensor", lambda e: e.transpose(
                    ptb(0, 128, 0, 8), ab(BROW, BROW + 8), ab(BIDENT, BIDENT + 128)),
                    wait=[tbf, bw("pt")], inc=PE)
                tev = p.emit("vector", lambda e, tt=tt: e.tensor_copy(
                    ab(BSROW + tt * 128, BSROW + (tt + 1) * 128, 0, 8), ptb(0, 128, 0, 8)),
                    wait=[ttr], inc=DV)
                bank_war["pt"] = tev
                bank_war["brow"] = ttr

            if FFN_GATE_ONLY:
                continue
            for th in range(2):
                tcol = idx * w + th * 512 if own else None
                for ei in range(FFN_NE):
                    esl = ei % 2
                    tsb = p.emit("tensor", lambda e, ei=ei, th=th: e.matmul(
                        bank(6), ab(BONEH + ei * 128, BONEH + (ei + 1) * 128, 0, 8),
                        ab(BSROW + th * 512, BSROW + (th + 1) * 512, 0, 8),
                        start=True, stop=True), wait=[p.tok(DV), bw(6)], inc=PE)
                    tsv = p.emit("vector", lambda e, esl=esl: e.tensor_copy(
                        ab(SBC + esl * 512, SBC + (esl + 1) * 512), bank(6)),
                        wait=[tsb, sbc_war[esl]], inc=DV)
                    setbw(6, tsv)
                    for hc in range(16):
                        wsl = hc % 4
                        woff = (SQB5 if wsl < 2 else W1S1) + (wsl % 2) * 1024
                        wl = [w1_war[wsl]]
                        if s5i == 0 and th == 0 and ei == 0 and hc < 4:
                            wl.append(t_precast)
                        if th == 0 and ei == 0 and hc < 2:
                            wl.append(t_apply[-1])
                        w1_tok[wsl] = p.dma("sync", lambda e, ei=ei, hc=hc, woff=woff: e.dma_start(
                            out=ab(woff, woff + 1024).rearrange("p (c i) -> p c i", c=NCH),
                            in_=dS["w1b"][ei].rearrange("(c p) i -> p c i", p=128)[:, :, hc * 128:(hc + 1) * 128]),
                            wait=wl, inc=SQ)
                        last = None
                        for d in range(NCH):
                            last = p.emit("tensor", lambda e, d=d, hc=hc, th=th, woff=woff: e.matmul(
                                bank(hc % 3),
                                ab(woff + d * 128, woff + d * 128 + 128),
                                ab(XLN5 + d * w + th * 512, XLN5 + d * w + (th + 1) * 512),
                                start=(d == 0), stop=(d == NCH - 1)),
                                wait=[w1_tok[wsl], t_xln_ready[d]] + ([bw(hc % 3)] if d == 0 else []),
                                inc=(PE if d == NCH - 1 else None))
                        w1_war[wsl] = last
                        tre = p.emit("scalar", lambda e, hc=hc, ei=ei: e.activation(
                            ab(HTM, HTM + 512), bank(hc % 3), AF.Relu,
                            bias=af(WCOL + 48 + ei * 16 + hc, WCOL + 48 + ei * 16 + hc + 1)),
                            wait=[last, htm_war[0]], inc=AC)
                        setbw(hc % 3, tre)
                        tmu = p.emit("vector", lambda e, hc=hc, esl=esl: e.tensor_mul(
                            ab(HPR + hc * 512, HPR + (hc + 1) * 512),
                            ab(HTM, HTM + 512), ab(SBC + esl * 512, SBC + (esl + 1) * 512)),
                            wait=[tre, tsv, w2done_prev], inc=DV)
                        htm_war[0] = tmu
                        hpr_tok[hc] = tmu
                    if DEBUG and s5i == 0 and th == 0 and ei == 0:
                        p.dma("gpsimd", lambda e: e.dma_start(
                            out=dS["dbg2"][0:8, 0:64].rearrange("a b -> a b"),
                            in_=ab(BSROW, BSROW + 64, 0, 8)), wait=[p.tok(DV)], inc=GQ)
                        p.dma("gpsimd", lambda e: e.dma_start(
                            out=dS["dbg"].rearrange("(c p) t -> p c t", p=128)[:, 0:2, :],
                            in_=ab(HPR, HPR + 8192).rearrange("p (c t) -> p c t", c=2)),
                            wait=[p.tok(DV)], inc=GQ)
                    for half in range(2):
                        for c in range(4):
                            wl2 = [w2_war[c]]
                            if s5i == 0 and th == 0 and ei == 0 and half == 0:
                                wl2.append(t_precast)
                            w2_tok[c] = p.dma("gpsimd", lambda e, ei=ei, half=half, c=c: e.dma_start(
                                out=ab(W2B + c * 2048, W2B + (c + 1) * 2048).rearrange("p (c2 i) -> p c2 i", c2=2),
                                in_=dS["w2b"][ei].rearrange("(c2 p) i -> p c2 i", p=128)[:, half * 8 + c * 2:half * 8 + c * 2 + 2, :]),
                                wait=wl2, inc=GQ)
                        for do in range(NCH):
                            yb = 4 + do % 2
                            first = (ei == 0 and half == 0)
                            tprev = None
                            if first:
                                tprev = p.emit("tensor", lambda e, do=do, th=th: e.matmul(
                                    bank(4 + do % 2),
                                    ab(B2B + do * 128, B2B + (do + 1) * 128, 0, 8),
                                    ab(BSROW + th * 512, BSROW + (th + 1) * 512, 0, 8),
                                    start=True, stop=False),
                                    wait=[p.tok(DV), bw(4 + do % 2)])
                            last = None
                            for hcq in range(8):
                                hc = half * 8 + hcq
                                csl = hcq // 2
                                give_tok = hcq == 7 or (do == NCH - 1 and hcq % 2 == 1)
                                last = p.emit("tensor", lambda e, do=do, hc=hc, hcq=hcq, csl=csl, first=first: e.matmul(
                                    bank(4 + do % 2),
                                    ab(W2B + csl * 2048 + (hcq % 2) * 1024 + do * 128,
                                       W2B + csl * 2048 + (hcq % 2) * 1024 + do * 128 + 128),
                                    ab(HPR + hc * 512, HPR + (hc + 1) * 512),
                                    start=(hcq == 0 and not first), stop=(hcq == 7)),
                                    wait=[w2_tok[csl], hpr_tok[hc]] +
                                         ([bw(4 + do % 2)] if hcq == 0 and not first else []),
                                    inc=(PE if give_tok else None))
                                if do == NCH - 1 and hcq % 2 == 1:
                                    w2_war[hcq // 2] = last
                            t_mm = last
                            if own:
                                dst = af(XRES + do * 2048 + tcol, XRES + do * 2048 + tcol + 512)
                            else:
                                dst = af(FBIG + do * w + th * 512, FBIG + do * w + (th + 1) * 512)
                            ta = p.emit("vector", lambda e, dst=dst, do=do: e.tensor_add(
                                dst, bank(4 + do % 2), dst), wait=[t_mm], inc=DV)
                            setbw(4 + do % 2, ta)
                    w2done_prev = p.tok(PE)
            if not own:
                tst = p.dma("sync", lambda e, idx=idx, dstp=dstp: e.dma_start(
                    out=dstp.rearrange("(d p) t -> p d t", p=128)[:, :, idx * w:(idx + 1) * w],
                    in_=af(FBIG, FBIG + 8192).rearrange("p (d t) -> p d t", d=NCH)),
                    wait=[p.tok(DV)], inc=SQ)
        return

    if N_PH < 3:
        finish(p, out, af)
        return

    SLOT = 12352
    P3K = BBIG
    P3Q = BBIG + 4096
    P3V = BBIG + 8192
    PTBUF = BBIG + 2 * SLOT
    OST = PTBUF + 4096
    assert OST + 1024 <= BF_COLS

    t_spill_done = p.tok(SQ)
    hp_load_tok = [None, None]
    hp_war = [None, None]

    def load_hp(hp):
        sl = hp % 2
        wl = [t_spill_done, hp_war[sl]]
        t1 = p.dma("sync", lambda e, hp=hp, sl=sl: e.dma_start(
            out=ab(P3K + sl * SLOT, P3K + sl * SLOT + 4096),
            in_=dS["k_sp"][hp * 128:(hp + 1) * 128, :]), wait=wl, inc=SQ)
        t2 = p.dma("sync", lambda e, hp=hp, sl=sl: e.dma_start(
            out=ab(P3Q + sl * SLOT, P3Q + sl * SLOT + 4096),
            in_=dS["q_sp"][hp * 128:(hp + 1) * 128, :]), inc=SQ)
        t3 = p.dma("sync", lambda e, hp=hp, sl=sl: e.dma_start(
            out=ab(P3V + sl * SLOT, P3V + sl * SLOT + 4160).rearrange("p (c f) -> p c f", c=32),
            in_=dS["v_sp"][hp].rearrange("(c p) f -> p c f", p=128)), inc=SQ)
        hp_load_tok[sl] = t3
        return t3

    load_hp(0)
    ost_war = [None, None]
    dn_war = [None, None]
    for hp in range(NHP):
        sl = hp % 2
        t_ld = hp_load_tok[sl]
        if hp + 1 < NHP:
            load_hp(hp + 1)
        last_pe_of_hp = None
        for q5 in range(8):
            t_pv = [None, None]
            texp_tok = [[None, None], [None, None]]
            tpv_tok = [[None, None], [None, None]]

            def emit_S(pair, hd, sl=sl, q5=q5, t_ld=t_ld):
                last = None
                for j in range(2):
                    k = 2 * pair + j
                    sb = hd * 2 + j
                    lkb = ab(P3K + sl * SLOT + k * 128, P3K + sl * SLOT + (k + 1) * 128,
                             64 * hd, 64 * hd + 64)
                    lqb = ab(P3Q + sl * SLOT + q5 * 512, P3Q + sl * SLOT + (q5 + 1) * 512,
                             64 * hd, 64 * hd + 64)
                    last = p.emit("tensor", lambda e, lkb=lkb, lqb=lqb, hd=hd, sb=sb: e.matmul(
                        bank(sb), lkb, lqb,
                        start=True, stop=True, tile_position=(64 * hd, 0)),
                        wait=[t_ld, bw(sb)], inc=(PE if j == 1 else None))
                return last

            def emit_exp(pair, hd, t_s, hp=hp):
                pslot = (pair % 2) * 2 + hd
                texp = p.emit("scalar", lambda e, pslot=pslot, hd=hd, hp=hp: e.activation(
                    ab(PTBUF + pslot * 1024, PTBUF + (pslot + 1) * 1024),
                    bank(hd * 2, 2), AF.Exp, scale=ATT_SCALE,
                    bias=af(MCOL + 2 * hp + hd, MCOL + 2 * hp + hd + 1)),
                    wait=[t_s, t_mcol, tpv_tok[pair % 2][hd]], inc=AC)
                setbw([hd * 2, hd * 2 + 1], texp)
                texp_tok[pair % 2][hd] = texp

            def emit_PV(pair, hd, sl=sl):
                pslot = (pair % 2) * 2 + hd
                last = None
                for j in range(2):
                    k = 2 * pair + j
                    lvb = ab(P3V + sl * SLOT + k * 130 + hd * 65,
                             P3V + sl * SLOT + k * 130 + hd * 65 + 65)
                    last = p.emit("tensor", lambda e, lvb=lvb, k=k, pslot=pslot, j=j, hd=hd: e.matmul(
                        bank(4 + hd, 1, 0, 65), lvb,
                        ab(PTBUF + pslot * 1024 + j * 512, PTBUF + pslot * 1024 + (j + 1) * 512),
                        start=(k == 0), stop=(k == 31)),
                        wait=[texp_tok[pair % 2][hd]] + ([bw(4 + hd)] if k == 0 else []),
                        inc=(PE if j == 1 else None))
                tpv_tok[pair % 2][hd] = last
                if pair == 15:
                    t_pv[hd] = last

            for pair in range(16):
                for hd in range(2):
                    t_s = emit_S(pair, hd)
                    emit_exp(pair, hd, t_s)
                if pair >= 1:
                    for hd in range(2):
                        emit_PV(pair - 1, hd)
            for hd in range(2):
                emit_PV(15, hd)
            osl = q5 % 2
            tms = []
            for hd in range(2):
                tdn = p.emit("vector", lambda e, hd=hd: e.tensor_copy(
                    af(SCR + hd * 512, SCR + hd * 512 + 512, 0, 1),
                    bank(4 + hd, 1, 64, 65)), wait=[t_pv[hd], dn_war[hd]], inc=DV)
                dn_war[hd] = p.dma("sync", lambda e, hd=hd, hp=hp, q5=q5: e.dma_start(
                    out=dS["dn_sp"][2 * hp + hd:2 * hp + hd + 1, q5 * 512:(q5 + 1) * 512],
                    in_=af(SCR + hd * 512, SCR + hd * 512 + 512, 0, 1)),
                    wait=[tdn], inc=SQ)
                tcp = p.emit("vector", lambda e, hd=hd, osl=osl: e.tensor_copy(
                    ab(OST + osl * 512, OST + (osl + 1) * 512, 64 * hd, 64 * hd + 64),
                    bank(4 + hd, 1, 0, 64)), wait=[t_pv[hd], ost_war[osl]], inc=DV)
                setbw(4 + hd, tcp)
                tms.append(tcp)
            tsp = p.dma("sync", lambda e, hp=hp, q5=q5, osl=osl: e.dma_start(
                out=dS["o_sp"][hp * 128:(hp + 1) * 128, q5 * 512:(q5 + 1) * 512],
                in_=ab(OST + osl * 512, OST + (osl + 1) * 512)), wait=tms, inc=SQ)
            ost_war[osl] = tsp
            last_pe_of_hp = p.tok(PE)
        hp_war[sl] = last_pe_of_hp

    if N_PH < 4:
        finish(p, out, af)
        return

    WPR = BBIG
    OBUF = BBIG + 8192
    assert OBUF + 16384 <= BF_COLS
    t_p3pe = p.tok(PE)
    t_p3dv = p.tok(DV)
    twp = p.dma("gpsimd", lambda e: e.dma_start(
        out=ab(WPR, WPR + 8192).rearrange("p (f m) -> p f m", f=NCH),
        in_=dI["wp"].rearrange("(f p) m -> p f m", p=128)), wait=[t_p3pe], inc=GQ)
    ob_tok = [None, None]
    ob_war = [None, None]

    def load_ob(sw):
        sl = sw % 2
        ob_tok[sl] = p.dma("sync", lambda e, sw=sw, sl=sl: e.dma_start(
            out=ab(OBUF + sl * 8192, OBUF + (sl + 1) * 8192).rearrange("p (f t) -> p f t", f=NCH),
            in_=dS["o_sp"].rearrange("(f p) t -> p f t", p=128)[:, :, sw * 1024:(sw + 1) * 1024]),
            wait=[ob_war[sl], p.tok(SQ) if sw == 0 else None], inc=SQ)

    load_ob(0)
    SEL2 = 1152
    rn_war = None
    rb_war = []
    for sw in range(4):
        own = sw < 2
        sl = sw % 2
        if sw + 1 < 4:
            load_ob(sw + 1)
        if not own:
            tpx = p.dma("sync", lambda e, sw=sw: e.dma_start(
                out=af(FBIG, FBIG + 8192).rearrange("p (d t) -> p d t", d=NCH),
                in_=dI["x0"].rearrange("(d p) t -> p d t", p=128)[:, :, TO + (sw - 2) * 1024: TO + (sw - 1) * 1024]),
                wait=[p.tok(DV)], inc=SQ)
        tdnl = p.dma("sync", lambda e, sw=sw: e.dma_start(
            out=af(SCR, SCR + 1024, 0, 16), in_=dS["dn_sp"][:, sw * 1024:(sw + 1) * 1024]),
            wait=[rn_war], inc=SQ)
        tdr = p.emit("vector", lambda e: e.reciprocal(
            af(SCR, SCR + 1024, 0, 16), af(SCR, SCR + 1024, 0, 16)), wait=[tdnl], inc=DV)
        tdc = p.emit("vector", lambda e: e.tensor_copy(
            ab(BROW, BROW + 1024, 0, 16), af(SCR, SCR + 1024, 0, 16)),
            wait=[tdr] + rb_war, inc=DV)
        rn_war = tdc
        rb_war = []
        rmul_tok = {}
        for f in range(NCH):
            for j in range(2):
                rb = 4 + ((2 * f + j) % 3)
                tm = p.emit("tensor", lambda e, f=f, j=j, rb=rb: e.matmul(
                    bank(rb), ab(SEL2 + f * 128, SEL2 + (f + 1) * 128, 0, 16),
                    ab(BROW + j * 512, BROW + (j + 1) * 512, 0, 16),
                    start=True, stop=True), wait=[tdc, bw(rb)], inc=PE)
                rb_war.append(tm)
                tmu = p.emit("vector", lambda e, f=f, j=j, rb=rb, sl=sl: e.tensor_mul(
                    ab(OBUF + sl * 8192 + f * 1024 + j * 512, OBUF + sl * 8192 + f * 1024 + (j + 1) * 512),
                    ab(OBUF + sl * 8192 + f * 1024 + j * 512, OBUF + sl * 8192 + f * 1024 + (j + 1) * 512),
                    bank(rb)), wait=[tm, ob_tok[sl]], inc=DV)
                setbw(rb, tmu)
                rmul_tok[(f, j)] = tmu
        last_mm_prev = None
        for do in range(NCH):
            pb = (do % 2) * 2
            last = None
            for f in range(NCH):
                for j in range(2):
                    last = p.emit("tensor", lambda e, do=do, f=f, j=j, sl=sl, pb=pb: e.matmul(
                        bank(pb, 2)[:, j * 512:(j + 1) * 512],
                        ab(WPR + f * 1024 + do * 128, WPR + f * 1024 + (do + 1) * 128),
                        ab(OBUF + sl * 8192 + f * 1024 + j * 512, OBUF + sl * 8192 + f * 1024 + (j + 1) * 512),
                        start=(f == 0), stop=(f == NCH - 1)),
                        wait=[rmul_tok[(f, j)], twp] + ([bw(pb)] if f == 0 and j == 0 else []),
                        inc=(PE if f == NCH - 1 and j == 1 else None))
            t_mm = last
            ob_war[sl] = t_mm
            tb = p.emit("scalar", lambda e, do=do, pb=pb: e.activation(
                af(BCA, BCA + 1024), bank(pb, 2), AF.Identity,
                bias=af(WCOL + do, WCOL + do + 1)), wait=[t_mm, bw("bca")], inc=AC)
            setbw(pb, tb)
            if own:
                ta = p.emit("vector", lambda e, do=do, sw=sw: e.tensor_add(
                    af(XRES + do * 2048 + sw * 1024, XRES + do * 2048 + (sw + 1) * 1024),
                    af(BCA, BCA + 1024),
                    af(XRES + do * 2048 + sw * 1024, XRES + do * 2048 + (sw + 1) * 1024)),
                    wait=[tb], inc=DV)
                bank_war["bca"] = ta
            else:
                ta = p.emit("vector", lambda e, do=do: e.tensor_add(
                    af(BCB, BCB + 1024), af(BCA, BCA + 1024),
                    af(FBIG + do * 1024, FBIG + (do + 1) * 1024)),
                    wait=[tb, tpx, bw("bcb")], inc=DV)
                bank_war["bca"] = ta
                tst = p.dma("sync", lambda e, do=do, sw=sw: e.dma_start(
                    out=dS["x1p"][do * 128:(do + 1) * 128, (sw - 2) * 1024:(sw - 1) * 1024],
                    in_=af(BCB, BCB + 1024)), wait=[ta], inc=SQ)
                bank_war["bcb"] = tst
    bank_war["brow"] = rb_war[-1]

    if N_PH < 5:
        finish(p, out, af)
        return

    ffn_phase([("own", 0), ("own", 1), ("pp", 0), ("pp", 1)][:FFN_SWEEPS], dS["x1p"], dS["x2p"])

    if N_PH < 6:
        finish(p, out, af)
        return

    HH6 = BBIG
    QKW6 = BBIG + 4096
    QKST = BBIG + 20480
    XL3B = BBIG + 22528
    SQ6 = BBIG + 26624
    QKB6 = BBIG + 27648
    assert QKB6 + 2048 <= BF_COLS

    t_all = [p.tok(PE), p.tok(DV), p.tok(AC), p.tok(SQ), p.tok(GQ)]
    p.dma("gpsimd", lambda e: e.dma_start(
        out=ab(QKW6, QKW6 + 16384).rearrange("p (d m) -> p d m", d=NCH),
        in_=dI["qkvw_t"].rearrange("(d p) m -> p d m", p=128)[:, :, 0:2048]),
        wait=t_all, inc=GQ)
    t_qkw = p.tok(GQ)
    tqb0 = p.dma("sync", lambda e: e.dma_start(
        out=af(FBIG + 4096, FBIG + 6144, 0, 1), in_=dI["qkvb_qk"][:, :]), wait=t_all, inc=SQ)
    tqb1 = p.emit("vector", lambda e: e.tensor_copy(
        ab(BROW, BROW + 2048, 0, 1), af(FBIG + 4096, FBIG + 6144, 0, 1)),
        wait=[tqb0, bw("brow")], inc=DV)
    for j in range(4):
        tm = p.emit("tensor", lambda e, j=j: e.matmul(
            bank(6), ones_row(), ab(BROW + j * 512, BROW + (j + 1) * 512, 0, 1),
            start=True, stop=True), wait=[tqb1, bw(6)], inc=PE)
        tv = p.emit("vector", lambda e, j=j: e.tensor_copy(
            ab(QKB6 + j * 512, QKB6 + (j + 1) * 512), bank(6)), wait=[tm], inc=DV)
        setbw(6, tv)
    bank_war["brow"] = p.tok(PE)
    t_qkb = p.tok(DV)

    def ln3_group(g, src_fn, wait_src, want_xln3):
        w6 = 512
        sq_mm_tok = [None, None]
        t_sum_mm = None
        t_sumsq_mm = None
        t_cast = []
        for d in range(NCH):
            slot = SQ6 + (d % 2) * w6
            tc = p.emit("vector", lambda e, d=d: e.tensor_copy(
                ab(XL3B + d * w6, XL3B + (d + 1) * w6), src_fn(d)), wait=wait_src, inc=DV)
            t_cast.append(tc)
            wl = list(wait_src) + ([sq_mm_tok[d % 2]] if d >= 2 else [bw("sq6")])
            ts = p.emit("scalar", lambda e, d=d, slot=slot: e.activation(
                ab(slot, slot + w6), src_fn(d), AF.Square), wait=wl, inc=AC)
            t_sum_mm = p.emit("tensor", lambda e, d=d: e.matmul(
                bank(2, 1, 0, 1), ones_col(),
                ab(XL3B + d * w6, XL3B + (d + 1) * w6),
                start=(d == 0), stop=(d == NCH - 1)),
                wait=[tc] + ([bw(2)] if d == 0 else []),
                inc=(PE if d == NCH - 1 else None))
            last = p.emit("tensor", lambda e, d=d, slot=slot: e.matmul(
                bank(4, 1, 0, 1), ones_col(), ab(slot, slot + w6),
                start=(d == 0), stop=(d == NCH - 1)),
                wait=[ts] + ([bw(4)] if d == 0 else []),
                inc=(PE if True else None))
            sq_mm_tok[d % 2] = last
            t_sumsq_mm = last
        w = w6
        t1 = p.emit("vector", lambda e: e.tensor_scalar(
            rows(R_MU, w), bank(2, 1, 0, 1), cst(C_INVD, 1), None, op0=ALU.mult),
            wait=[t_sum_mm], inc=DV)
        t2 = p.emit("vector", lambda e: e.tensor_scalar(
            rows(R_SQ, w), bank(4, 1, 0, 1), cst(C_INVD, 1), None, op0=ALU.mult),
            wait=[t_sumsq_mm], inc=DV)
        setbw(2, t1)
        setbw(4, t2)
        t3 = p.emit("vector", lambda e: e.tensor_mul(rows(R_TMP, w), rows(R_MU, w), rows(R_MU, w)),
                    wait=[t1], inc=DV)
        t4 = p.emit("vector", lambda e: e.tensor_sub(rows(R_A, w), rows(R_SQ, w), rows(R_TMP, w)),
                    wait=[t2, t3], inc=DV)
        t5 = p.emit("scalar", lambda e: e.activation(rows(R_B, w), rows(R_A, w), AF.Ln, bias=cst(C_EPS, 1)),
                    wait=[t4], inc=AC)
        t6 = p.emit("scalar", lambda e: e.activation(rows(R_SQ, w), rows(R_B, w), AF.Exp, scale=-0.5),
                    wait=[t5, t2], inc=AC)
        r7 = p.emit("vector", lambda e: e.tensor_scalar(
            rows(R_TMP, w), rows(R_A, w), cst(C_EPS, 1), None, op0=ALU.add), wait=[t4, t3], inc=DV)
        r8 = p.emit("vector", lambda e: e.reciprocal(rows(R_B, w), rows(R_TMP, w)), wait=[r7, t6], inc=DV)
        r9 = p.emit("vector", lambda e: e.tensor_mul(rows(R_TMP, w), rows(R_A, w), rows(R_B, w)),
                    wait=[r8], inc=DV)
        r10 = p.emit("scalar", lambda e: e.activation(rows(R_B, w), rows(R_TMP, w), AF.Ln, bias=cst(C_EPS, 1)),
                     wait=[r9], inc=AC)
        r11 = p.emit("scalar", lambda e: e.activation(rows(R_TMP, w), rows(R_B, w), AF.Exp, scale=-0.5),
                     wait=[r10], inc=AC)
        r12 = p.emit("vector", lambda e: e.tensor_mul(rows(R_B, w), rows(R_MU, w), rows(R_SQ, w)),
                     wait=[t6, r11], inc=DV)
        r13 = p.emit("vector", lambda e: e.tensor_scalar(
            rows(R_A, w), rows(R_B, w), cst(C_NEG1, 1), None, op0=ALU.mult), wait=[r12], inc=DV)
        t_bc = []
        for (roff, dst, tr, boff) in ((R_SQ, BCA, r12, 0), (R_A, BCB, r13, 512),
                                      (R_TMP, FBIG + 4096, r13, 1024)):
            tcast = p.emit("vector", lambda e, roff=roff, boff=boff: e.tensor_copy(
                ab(BROW + boff, BROW + boff + w, 0, 1), rows(roff, w)),
                wait=[tr, bw("brow")], inc=DV)
            tm = p.emit("tensor", lambda e, boff=boff: e.matmul(
                bank(6), ones_row(), ab(BROW + boff, BROW + boff + w, 0, 1),
                start=True, stop=True), wait=[tcast, bw(6)], inc=PE)
            tv = p.emit("vector", lambda e, dst=dst: e.tensor_copy(
                af(dst, dst + w), bank(6)), wait=[tm], inc=DV)
            setbw(6, tv)
            t_bc.append(tv)
        bank_war["brow"] = t_bc[-1]
        t_hh = []
        t_x3 = []
        for d in range(NCH):
            m1 = p.emit("vector", lambda e, d=d: e.tensor_mul(
                af(SCR, SCR + w), src_fn(d), af(BCA, BCA + w)),
                wait=list(t_bc) + ([t_hh[d - 1]] if d > 0 else []), inc=DV)
            m2 = p.emit("vector", lambda e, d=d: e.tensor_add(
                af(SCR + 512, SCR + 1024), af(SCR, SCR + w), af(BCB, BCB + w)),
                wait=[m1], inc=DV)
            if want_xln3:
                t_x3.append(p.emit("vector", lambda e, d=d: e.tensor_copy(
                    ab(XL3B + d * w, XL3B + (d + 1) * w), af(SCR + 512, SCR + 1024)),
                    wait=[m2], inc=DV))
            m3 = p.emit("vector", lambda e, d=d: e.tensor_mul(
                af(SCR, SCR + w), af(SCR + 512, SCR + 1024), af(FBIG + 4096, FBIG + 4096 + w)),
                wait=[m2], inc=DV)
            t_hh.append(p.emit("scalar", lambda e, d=d: e.activation(
                ab(HH6 + d * w, HH6 + (d + 1) * w), af(SCR, SCR + w), AF.Identity,
                scale=af(WCOL + 24 + d, WCOL + 24 + d + 1),
                bias=af(WCOL + 32 + d, WCOL + 32 + d + 1)), wait=[m3], inc=AC))
        bank_war["sq6"] = t_sumsq_mm
        return t_hh, t_x3

    qkst_war = [None, None]
    hh_war = None
    spill_war = []
    x2src_war = None
    for g in range(8):
        own = g < 4
        w6 = 512
        if own:
            def src_fn(d, g=g):
                return af(XRES + d * 2048 + g * 512, XRES + d * 2048 + (g + 1) * 512)
            t_src = [p.tok(DV)]
        else:
            tl = p.dma("sync", lambda e, g=g: e.dma_start(
                out=af(FBIG, FBIG + 4096).rearrange("p (d t) -> p d t", d=NCH),
                in_=dS["x2p"].rearrange("(d p) t -> p d t", p=128)[:, :, (g - 4) * 512:(g - 3) * 512]),
                wait=[p.tok(DV), p.tok(AC), x2src_war], inc=SQ)

            def src_fn(d):
                return af(FBIG + d * 512, FBIG + (d + 1) * 512)
            t_src = [tl]
        if hh_war is not None:
            t_src = t_src + [hh_war] + spill_war
        t_hh, t_x3 = ln3_group(g, src_fn, t_src, own)
        spill_war = []
        if own:
            ts1 = p.dma("sync", lambda e, g=g: e.dma_start(
                out=dS["xln3o"].rearrange("(d p) t -> p d t", p=128)[:, :, g * 512:(g + 1) * 512],
                in_=ab(XL3B, XL3B + 4096).rearrange("p (d t) -> p d t", d=NCH)),
                wait=[t_x3[-1]], inc=SQ)
            ts2 = p.dma("sync", lambda e, g=g: e.dma_start(
                out=dS["hh_sp"].rearrange("(d p) t -> p d t", p=128)[:, :, g * 512:(g + 1) * 512],
                in_=ab(HH6, HH6 + 4096).rearrange("p (d t) -> p d t", d=NCH)),
                wait=[t_hh[-1]], inc=SQ)
            spill_war = [ts1, ts2]
        last_hh_mm = None
        for tt in range(4):
            last = None
            for d in range(NCH):
                for j in range(4):
                    last = p.emit("tensor", lambda e, d=d, j=j, tt=tt: e.matmul(
                        bank(j), ab(HH6 + d * 512 + tt * 128, HH6 + d * 512 + tt * 128 + 128),
                        ab(QKW6 + d * 2048 + j * 512, QKW6 + d * 2048 + (j + 1) * 512),
                        start=(d == 0), stop=(d == NCH - 1)),
                        wait=[t_hh[d], t_qkw] + ([bw(j)] if d == 0 else []),
                        inc=(PE if d == NCH - 1 and j == 3 else None))
            t_mm = last
            last_hh_mm = t_mm
            ssl = tt % 2
            tv = p.emit("vector", lambda e, ssl=ssl: e.tensor_add(
                ab(QKST + ssl * 1024, QKST + ssl * 1024 + 1024),
                bank(0, 2), ab(QKB6, QKB6 + 1024)),
                wait=[t_mm, t_qkb, qkst_war[ssl]], inc=DV)
            setbw([0, 1], tv)
            td = p.dma("sync", lambda e, g=g, tt=tt, ssl=ssl: e.dma_start(
                out=dS["qb_sp"][g * 512 + tt * 128: g * 512 + (tt + 1) * 128, :],
                in_=ab(QKST + ssl * 1024, QKST + ssl * 1024 + 1024)), wait=[tv], inc=SQ)
            tvk = p.emit("vector", lambda e, ssl=ssl: e.tensor_add(
                ab(QKST + ssl * 1024, QKST + ssl * 1024 + 1024),
                bank(2, 2), ab(QKB6 + 1024, QKB6 + 2048)),
                wait=[t_mm, td], inc=DV)
            setbw([2, 3], tvk)
            tdk = p.dma("sync", lambda e, g=g, tt=tt, ssl=ssl: e.dma_start(
                out=dS["kb_sp"][g * 512 + tt * 128: g * 512 + (tt + 1) * 128, :],
                in_=ab(QKST + ssl * 1024, QKST + ssl * 1024 + 1024)), wait=[tvk], inc=SQ)
            qkst_war[ssl] = tdk
        hh_war = last_hh_mm

    VW6 = BBIG + 4096
    HHG = BBIG + 12288
    VST6 = BBIG + 16384
    t_pa = [p.tok(PE), p.tok(DV), p.tok(AC), p.tok(SQ)]
    t_vw = p.dma("gpsimd", lambda e: e.dma_start(
        out=ab(VW6, VW6 + 8192).rearrange("p (d m) -> p d m", d=NCH),
        in_=dI["qkvw_t"].rearrange("(d p) m -> p d m", p=128)[:, :, 2048:3072]),
        wait=t_pa, inc=GQ)
    vst_war6 = [None, None]
    hhg_tok = None
    hhg_war = None
    for g in range(4):
        tl = p.dma("sync", lambda e, g=g: e.dma_start(
            out=ab(HHG, HHG + 4096).rearrange("p (d t) -> p d t", d=NCH),
            in_=dS["hh_sp"].rearrange("(d p) t -> p d t", p=128)[:, :, g * 512:(g + 1) * 512]),
            wait=t_pa + [hhg_war], inc=SQ)
        last_v = None
        for vf in range(8):
            vb_ = 4 + vf % 2
            last = None
            for d in range(NCH):
                last = p.emit("tensor", lambda e, d=d, vf=vf, vb_=vb_: e.matmul(
                    bank(vb_), ab(VW6 + d * 1024 + vf * 128, VW6 + d * 1024 + (vf + 1) * 128),
                    ab(HHG + d * 512, HHG + (d + 1) * 512),
                    start=(d == 0), stop=(d == NCH - 1)),
                    wait=[tl, t_vw] + ([bw(vb_)] if d == 0 else []),
                    inc=(PE if d == NCH - 1 else None))
            t_mm = last
            last_v = t_mm
            ssl = vf % 2
            tv = p.emit("scalar", lambda e, vf=vf, ssl=ssl, vb_=vb_: e.activation(
                ab(VST6 + ssl * 512, VST6 + (ssl + 1) * 512), bank(vb_), AF.Identity,
                bias=af(WCOL + 8 + vf, WCOL + 8 + vf + 1)),
                wait=[t_mm, vst_war6[ssl]], inc=AC)
            setbw(vb_, tv)
            td = p.dma("sync", lambda e, g=g, vf=vf, ssl=ssl: e.dma_start(
                out=dS["vb_sp"][vf * 128:(vf + 1) * 128, g * 512:(g + 1) * 512],
                in_=ab(VST6 + ssl * 512, VST6 + (ssl + 1) * 512)), wait=[tv], inc=SQ)
            vst_war6[ssl] = td
        hhg_war = last_v

    QG6 = BBIG
    KG6 = BBIG + 8192
    t_pb = [p.tok(PE), p.tok(DV), p.tok(AC), p.tok(SQ)]
    qg_tok = [None, None]
    qg_war = [None, None]
    lg_tok = {}

    def load_qk(g):
        sl = g % 2
        t1 = p.dma("sync", lambda e, g=g, sl=sl: e.dma_start(
            out=ab(QG6 + sl * 4096, QG6 + (sl + 1) * 4096).rearrange("p (c m) -> p c m", c=4),
            in_=dS["qb_sp"].rearrange("(c p) m -> p c m", p=128)[:, g * 4:(g + 1) * 4, :]),
            wait=t_pb + [qg_war[sl]], inc=SQ)
        t2 = p.dma("sync", lambda e, g=g, sl=sl: e.dma_start(
            out=ab(KG6 + sl * 4096, KG6 + (sl + 1) * 4096).rearrange("p (c m) -> p c m", c=4),
            in_=dS["kb_sp"].rearrange("(c p) m -> p c m", p=128)[:, g * 4:(g + 1) * 4, :]),
            inc=SQ)
        qg_tok[sl] = t2

    load_qk(0)
    for g in range(8):
        sl = g % 2
        t_ld = qg_tok[sl]
        if g + 1 < 8:
            load_qk(g + 1)
        last_of_g = None
        for qf in range(8):
            pb6 = (qf % 2) * 2
            last = None
            for tt in range(4):
                for j in range(2):
                    last = p.emit("tensor", lambda e, qf=qf, tt=tt, j=j, sl=sl, pb6=pb6: e.matmul(
                        bank(pb6, 2)[:, j * 512:(j + 1) * 512],
                        ab(QG6 + sl * 4096 + tt * 1024 + qf * 128, QG6 + sl * 4096 + tt * 1024 + (qf + 1) * 128),
                        ab(KG6 + sl * 4096 + tt * 1024 + j * 512, KG6 + sl * 4096 + tt * 1024 + (j + 1) * 512),
                        start=(tt == 0), stop=(tt == 3)),
                        wait=[t_ld] + ([bw(pb6)] if tt == 0 and j == 0 else []),
                        inc=(PE if tt == 3 and j == 1 else None))
            t_mm = last
            if g == 0:
                ta = p.emit("vector", lambda e, qf=qf, pb6=pb6: e.tensor_copy(
                    af(FBIG + qf * 1024, FBIG + (qf + 1) * 1024), bank(pb6, 2)),
                    wait=[t_mm], inc=DV)
            else:
                ta = p.emit("vector", lambda e, qf=qf, pb6=pb6: e.tensor_add(
                    af(FBIG + qf * 1024, FBIG + (qf + 1) * 1024), bank(pb6, 2),
                    af(FBIG + qf * 1024, FBIG + (qf + 1) * 1024)),
                    wait=[t_mm, lg_tok.get(qf)], inc=DV)
            lg_tok[qf] = ta
            setbw([pb6, pb6 + 1], ta)
            last_of_g = t_mm
        qg_war[sl] = last_of_g

    W6 = BBIG + 16384
    WT6 = BBIG
    sm_tok = {}
    for qf in range(8):
        c0 = SCR
        tmx = p.emit("vector", lambda e, qf=qf: e.reduce_max(
            af(SCR, SCR + 1), af(FBIG + qf * 1024, FBIG + (qf + 1) * 1024), axis=AX.X),
            wait=[lg_tok[qf]], inc=DV)
        tng = p.emit("vector", lambda e: e.tensor_scalar(
            af(SCR + 1, SCR + 2), af(SCR, SCR + 1), cst(C_NATTB), None, op0=ALU.mult),
            wait=[tmx], inc=DV)
        tex = p.emit("scalar", lambda e, qf=qf: e.activation(
            ab(W6 + qf * 1024, W6 + (qf + 1) * 1024), af(FBIG + qf * 1024, FBIG + (qf + 1) * 1024),
            AF.Exp, scale=ATTB_SCALE, bias=af(SCR + 1, SCR + 2),
            accum_out=af(SCR + 2, SCR + 3)), wait=[tng, bw("w6")], inc=AC)
        trc = p.emit("vector", lambda e: e.reciprocal(
            af(SCR + 3, SCR + 4), af(SCR + 2, SCR + 3)), wait=[tex], inc=DV)
        tsc = p.emit("vector", lambda e, qf=qf: e.tensor_scalar(
            ab(W6 + qf * 1024, W6 + (qf + 1) * 1024), ab(W6 + qf * 1024, W6 + (qf + 1) * 1024),
            af(SCR + 3, SCR + 4), None, op0=ALU.mult), wait=[trc], inc=DV)
        sm_tok[qf] = tsc
    wt_tok = {}
    for kf in range(8):
        for qf in range(8):
            ttr = p.emit("tensor", lambda e, kf=kf, qf=qf: e.transpose(
                ptb(0, 128), ab(W6 + qf * 1024 + kf * 128, W6 + qf * 1024 + (kf + 1) * 128),
                ab(BIDENT, BIDENT + 128)), wait=[sm_tok[qf], bw("pt")], inc=PE)
            tev = p.emit("vector", lambda e, kf=kf, qf=qf: e.tensor_copy(
                ab(WT6 + kf * 1024 + qf * 128, WT6 + kf * 1024 + (qf + 1) * 128), ptb(0, 128)),
                wait=[ttr], inc=DV)
            bank_war["pt"] = tev
            wt_tok[kf] = tev
    bank_war["w6"] = p.tok(DV)

    VB6 = BBIG + 8192
    ACH = BBIG + 12288
    PJW = BBIG + 16384
    XL36 = BBIG + 24576
    t_pd = [p.tok(PE), p.tok(DV), p.tok(AC), p.tok(SQ)]
    t_pj = p.dma("gpsimd", lambda e: e.dma_start(
        out=ab(PJW, PJW + 8192).rearrange("p (f m) -> p f m", f=NCH),
        in_=dI["projw_t"].rearrange("(f p) m -> p f m", p=128)), wait=t_pd, inc=GQ)
    vb_war = None
    xl_war = None
    for cc in range(4):
        tlv = p.dma("sync", lambda e, cc=cc: e.dma_start(
            out=ab(VB6, VB6 + 4096).rearrange("p (f t) -> p f t", f=NCH),
            in_=dS["vb_sp"].rearrange("(f p) t -> p f t", p=128)[:, :, cc * 512:(cc + 1) * 512]),
            wait=t_pd + [vb_war], inc=SQ)
        tlx = p.dma("sync", lambda e, cc=cc: e.dma_start(
            out=ab(XL36, XL36 + 4096).rearrange("p (d t) -> p d t", d=NCH),
            in_=dS["xln3o"].rearrange("(d p) t -> p d t", p=128)[:, :, cc * 512:(cc + 1) * 512]),
            wait=[xl_war] if xl_war else [], inc=SQ)
        ach_tok = {}
        last_av = None
        for qf in range(8):
            ab6 = qf % 2
            last = None
            for kf in range(8):
                last = p.emit("tensor", lambda e, qf=qf, kf=kf, ab6=ab6: e.matmul(
                    bank(ab6), ab(WT6 + kf * 1024 + qf * 128, WT6 + kf * 1024 + (qf + 1) * 128),
                    ab(VB6 + kf * 512, VB6 + (kf + 1) * 512),
                    start=(kf == 0), stop=(kf == NCH - 1)),
                    wait=[tlv, wt_tok[kf]] + ([bw(ab6)] if kf == 0 else []),
                    inc=(PE if kf == NCH - 1 else None))
            t_mm = last
            last_av = t_mm
            tv = p.emit("vector", lambda e, qf=qf, ab6=ab6: e.tensor_copy(
                ab(ACH + qf * 512, ACH + (qf + 1) * 512), bank(ab6)),
                wait=[t_mm, bw("ach")], inc=DV)
            setbw(ab6, tv)
            ach_tok[qf] = tv
        vb_war = last_av
        last_pj = None
        for do in range(NCH):
            pb6 = 2 + do % 2
            last = None
            for f in range(NCH):
                last = p.emit("tensor", lambda e, do=do, f=f, pb6=pb6: e.matmul(
                    bank(pb6), ab(PJW + f * 1024 + do * 128, PJW + f * 1024 + (do + 1) * 128),
                    ab(ACH + f * 512, ACH + (f + 1) * 512),
                    start=(f == 0), stop=(f == NCH - 1)),
                    wait=[t_pj, ach_tok[f]] + ([bw(pb6)] if f == 0 else []),
                    inc=(PE if f == NCH - 1 else None))
            t_mm = last
            last_pj = t_mm
            tb = p.emit("scalar", lambda e, do=do, pb6=pb6: e.activation(
                af(BCA, BCA + 512), bank(pb6), AF.Identity,
                bias=af(WCOL + 16 + do, WCOL + 16 + do + 1)), wait=[t_mm, bw("bca")], inc=AC)
            setbw(pb6, tb)
            t5a = p.emit("vector", lambda e, do=do: e.tensor_add(
                af(BCB, BCB + 512), af(BCA, BCA + 512), ab(XL36 + do * 512, XL36 + (do + 1) * 512)),
                wait=[tb, tlx, bw("bcb")], inc=DV)
            bank_war["bca"] = t5a
            t5b = p.emit("vector", lambda e, do=do, cc=cc: e.tensor_add(
                af(XRES + do * 2048 + cc * 512, XRES + do * 2048 + (cc + 1) * 512),
                af(BCB, BCB + 512),
                af(XRES + do * 2048 + cc * 512, XRES + do * 2048 + (cc + 1) * 512)),
                wait=[t5a], inc=DV)
            bank_war["bcb"] = t5b
        xl_war = last_pj

    if N_PH < 7:
        finish(p, out, af)
        return

    ffn_phase([("own", 0), ("own", 1)], None, None)

    finish(p, out, af)


def finish(p, out, af):
    td = p.dma("sync", lambda e: e.dma_start(
        out=out.rearrange("(d p) t -> p d t", p=128),
        in_=af(XRES, XRES + 16384).rearrange("p (d t) -> p d t", d=NCH)),
        wait=[p.tok("pe"), p.tok("dv"), p.tok("ac"), p.tok("sq"), p.tok("gq")], inc="sq")
    p.emit("sync", lambda e: None, wait=[td])


def _layout_inputs(inputs, core):
    b, h = core // 2, core % 2
    x = np.asarray(inputs["L0_attn_a_query"])
    xT = np.ascontiguousarray(x[b].T)
    x0 = np.concatenate([xT[:, h * TO:(h + 1) * TO], xT[:, (1 - h) * TO:(2 - h) * TO]], axis=1)
    m = {"x0": np.ascontiguousarray(x0)}
    m["wq"] = np.ascontiguousarray(np.asarray(inputs["L0_attn_a_Wq"]).transpose(1, 0, 2).reshape(D, D))
    m["wk"] = np.ascontiguousarray(np.asarray(inputs["L0_attn_a_Wk"]).transpose(1, 0, 2).reshape(D, D))
    m["wv"] = np.ascontiguousarray(np.asarray(inputs["L0_attn_a_Wv"]).transpose(1, 0, 2).reshape(D, D))
    m["wp"] = np.ascontiguousarray(
        np.asarray(inputs["L0_attn_a_Wp"]).reshape(EH, 16, D).transpose(1, 0, 2).reshape(D, D))
    m["bp_cols"] = np.ascontiguousarray(np.asarray(inputs["L0_attn_a_bp"]).reshape(8, 128).T)
    m["qkvw_t"] = np.ascontiguousarray(np.asarray(inputs["L0_attn_b_qkv_w"]).T)
    m["qkvb_qk"] = np.ascontiguousarray(np.asarray(inputs["L0_attn_b_qkv_b"])[None, 0:2 * D])
    m["qkvb_v_cols"] = np.ascontiguousarray(np.asarray(inputs["L0_attn_b_qkv_b"])[2 * D:].reshape(8, 128).T)
    m["projw_t"] = np.ascontiguousarray(np.asarray(inputs["L0_attn_b_proj_w"]).T)
    m["projb_cols"] = np.ascontiguousarray(np.asarray(inputs["L0_attn_b_proj_b"]).reshape(8, 128).T)
    m["nw_cols"] = np.ascontiguousarray(np.asarray(inputs["L0_attn_b_norm_w"]).reshape(8, 128).T)
    m["nb_cols"] = np.ascontiguousarray(np.asarray(inputs["L0_attn_b_norm_b"]).reshape(8, 128).T)
    m["wg"] = np.ascontiguousarray(np.asarray(inputs["L0_ffn_Wg"]))
    m["w1"] = np.ascontiguousarray(np.asarray(inputs["L0_ffn_W1"]))
    m["b1_cols"] = np.ascontiguousarray(
        np.asarray(inputs["L0_ffn_b1"]).reshape(NE, 16, 128).transpose(2, 0, 1).reshape(128, 128))
    m["w2"] = np.ascontiguousarray(np.asarray(inputs["L0_ffn_W2"]))
    m["b2r"] = np.ascontiguousarray(np.asarray(inputs["L0_ffn_b2"]))
    m["ident"] = np.eye(128, dtype=ml_dtypes.bfloat16)
    oh = np.zeros((NE, NE * 128), dtype=ml_dtypes.bfloat16)
    for e in range(NE):
        oh[e, e * 128:(e + 1) * 128] = 1
    m["onehot"] = oh
    s2 = np.zeros((16, 1024), dtype=ml_dtypes.bfloat16)
    for hp in range(8):
        s2[2 * hp, hp * 128:hp * 128 + 64] = 1
        s2[2 * hp + 1, hp * 128 + 64:hp * 128 + 128] = 1
    m["sel2"] = s2
    return m


def kernel(**inputs):
    nc = build_nc()
    in_maps = [_layout_inputs(inputs, c) for c in range(8)]
    res = run_bass_kernel_spmd(nc, in_maps, core_ids=list(range(8)))
    y = np.zeros((4, TF, D), dtype=np.float32)
    for c in range(8):
        b, h = c // 2, c % 2
        y[b, h * TO:(h + 1) * TO, :] = np.asarray(res.results[c]["out"]).T
    return y
